# version 20
# baseline (speedup 1.0000x reference)
import os
import numpy as np
from contextlib import ExitStack

import concourse.bass as bass
import concourse.mybir as mybir
from concourse.bass_utils import run_bass_kernel_spmd

F32 = mybir.dt.float32
BF16 = mybir.dt.bfloat16
AF = mybir.ActivationFunctionType
ALU = mybir.AluOpType
AX = mybir.AxisListType

DEBUG = bool(int(os.environ.get("MK_DEBUG", "0")))
STOP = os.environ.get("MK_STOP", "")

D = 1024
TLOC = 2048
NSC = 8
SCW = 256
NH = 16
HD = 64
NEC = 8
NEXP = 16
I_EXP = 1408
NIC = 11
NTT = 16
NPSEUDO = 18
DELTA = float(2.0 ** -40)

A_KFEAT = 0
A_RG = 2048
A_PA = 4096
A_PBH = 5120
A_PQ = 6144
A_PBL = 7168
A_PDT = 7680
A_STG = 8704
A_S = 10752
A_TF = 11264
A_MROW = 11776
A_VROW = 13824
A_MSQ = 15872
C_CRA = 11776
C_CRBH = 11904
C_CRQ = 12032
C_CRBL = 12160
C_TDROW = 12224
B_MB = 16384
B_RSB = 18432
E_XE = 16384
E_SQ = 18432
E_OUTS = 19456
C_SCORES = 21504
C_CW = 21760
C_GBB = 22016
C_GW = 22032
C_TD = 22160
C_TF = 22168
C_TM = 22176
C_WCC = 22184
C_ONESC = 22192
C_ONESR = 22193
C_EPSC = 22321
C_TRIL = 22322
C_TRIH = 22386
C_R16 = 22450
C_GR4 = 22466
C_IDT = 22470
C_CJC = 22598
C_GBROW = 22599
C_Z64 = 22615
GS = 22743
L2_C1 = GS + 0
L2_C2 = GS + 1
L2_M = GS + 2
L2_T = GS + 3
L2_V = GS + 4
L2_SD = GS + 5
C_CJC2 = GS + 6
C_MC = GS + 16
C_VC = GS + 32
C_RSC = GS + 48
C_SD2 = GS + 64
C_GB2 = GS + 80
C_GROW = GS + 96
C_NEG1 = GS + 112
F32_N = GS + 128
GT_SFC = 16384
GT_KEY = 16640
GT_GS = 16896
GT_GK = 16960
GT_CNT = 17024
GT_SEL = 17088
GT_CWU = 17152
GT_WS = 17408
GT_WI = 17424

B_WK = 0
B_WV = 8192
B_WR = 16384
B_WG = 24576
B_WO = 32768
B_XXB = 40960
B_VTOK = 43008
B_KHAT = 45056
B_VFEAT = 47104
B_BL = 49152
B_BH = 50176
B_KT = 52224
B_APAIR = 54272
B_ATM = 55296
B_AOB = 55552
ATTN_B16_END = B_AOB + 2048
B_X1B = 0
B_ACT = 16384
B_WGU = 38912
B_WD = 45056
B_SILU = 56320
B_SB = 57600
B_XXB2 = 58112
B16_N = 60160
assert ATTN_B16_END <= B16_N
assert F32_N * 4 + B16_N * 2 <= 207 * 1024, (F32_N * 4 + B16_N * 2)


class Planner:
    ENGINES = ("sync", "gpsimd", "scalar", "vector", "tensor")

    def __init__(self):
        self.q = {e: [] for e in self.ENGINES}
        self.cnt = {}
        self.waited = {e: {} for e in self.ENGINES}

    def wait(self, eng, marker):
        if marker is None:
            return
        s, v = marker
        if self.waited[eng].get(s, 0) >= v:
            return
        self.waited[eng][s] = v
        q = self.q[eng]
        if q and q[-1][0] == "w" and q[-1][1] == s:
            q[-1] = ("w", s, max(q[-1][2], v))
        else:
            q.append(("w", s, v))

    def emit(self, eng, fn, *args, _inc=None, _incn=1, **kwargs):
        self.q[eng].append(("i", fn, args, kwargs, _inc, _incn))
        if _inc:
            self.cnt[_inc] = self.cnt.get(_inc, 0) + _incn
            return (_inc, self.cnt[_inc])
        return None

    def marker(self, sem):
        return (sem, self.cnt.get(sem, 0))

    def fence(self, eng, sem):
        for idx in range(len(self.q[eng]) - 1, -1, -1):
            item = self.q[eng][idx]
            if item[0] == "i":
                if item[4] is None:
                    self.cnt[sem] = self.cnt.get(sem, 0) + 1
                    self.q[eng][idx] = (item[0], item[1], item[2], item[3], sem, 1)
                    mk = (sem, self.cnt[sem])
                else:
                    mk = (item[4], self.cnt[item[4]])
                self.wait(eng, mk)
                return mk
        return None


class Bank:
    __slots__ = ("ap", "w", "r")

    def __init__(self, ap):
        self.ap = ap
        self.w = None
        self.r = None


class Res:
    __slots__ = ("w", "r")

    def __init__(self):
        self.w = None
        self.r = None


def build_nc():
    nc = bass.Bass("TRN2", target_bir_lowering=False)
    g = {}

    def din(name, shape, dt=F32):
        g[name] = nc.dram_tensor(name, shape, dt, kind="ExternalInput").ap()

    din("xT", [D, TLOC]); din("xTpre", [D, TLOC])
    for w in ("wkT", "wvT", "wrT", "wgT", "woT"):
        din(w, [D, D])
    din("tdc", [NEC, 128]); din("tfc", [NEC, 128]); din("tmc", [NEC, 128]); din("tdrow", [1, D])
    din("gwT", [D, 16]); din("gbrow", [1, 16])
    din("gpT", [NEXP, D, I_EXP]); din("upT", [NEXP, D, I_EXP]); din("dpT", [NEXP, I_EXP, D])
    din("sgT", [D, 2816]); din("suT", [D, 2816]); din("sdT", [2816, D])
    din("cra", [128, 128]); din("crbh", [128, 128]); din("crq", [128, 128]); din("crbl", [128, 64])
    din("cjc", [128, 1]); din("ctril", [64, 64]); din("ctrih", [128, 64])
    g["ctrili"] = nc.dram_tensor("ctrili", [64, 64], mybir.dt.uint8, kind="ExternalInput").ap()
    g["ctrihi"] = nc.dram_tensor("ctrihi", [128, 64], mybir.dt.uint8, kind="ExternalInput").ap()
    g["ctrif"] = nc.dram_tensor("ctrif", [128, 128], mybir.dt.uint8, kind="ExternalInput").ap()
    din("cr16", [128, 16]); din("cgr4", [128, 4]); din("cidt", [128, 128])
    out = nc.dram_tensor("out", [TLOC, D], F32, kind="ExternalOutput").ap()
    zkind = "ExternalOutput" if DEBUG else "Internal"
    x1bsp = nc.dram_tensor("x1bsp", [D, TLOC], BF16, kind=zkind).ap()
    x1sp = nc.dram_tensor("x1sp", [D, TLOC], F32, kind=zkind).ap()

    ctx = ExitStack()
    FB = ctx.enter_context(nc.sbuf_tensor("FB", [128, F32_N], F32))
    BB = ctx.enter_context(nc.sbuf_tensor("BB", [128, B16_N], BF16))
    pb = [ctx.enter_context(nc.psum_tensor("pb%d" % i, [128, 512], F32)) for i in range(8)]
    MSK = ctx.enter_context(nc.sbuf_tensor("MSK", [128, 256], mybir.dt.uint8))
    semnames = (["mm", "dv", "ac", "cn", "cn2", "cnp", "wk", "wv", "wr", "wg", "wo",
                 "sg0", "sg1", "sp0", "sp1", "xe0", "xe1",
                 "ou0", "ou1", "dbg", "gp"]
                + ["g%d" % i for i in range(6)] + ["d%d" % i for i in range(11)]
                + ["rl%d" % i for i in range(8)] + ["xr%d" % i for i in range(8)])
    sems = {s: ctx.enter_context(nc.semaphore(s)) for s in semnames}

    p = Planner()
    banks = [Bank(pb[i]) for i in range(8)]

    def fv(off, w, p0=0, pn=128):
        return FB[p0:p0 + pn, off:off + w]

    def bv(off, w, p0=0, pn=128):
        return BB[p0:p0 + pn, off:off + w]

    def dma(queue, out_ap, in_ap, sem):
        return p.emit(queue, "dma_start", _inc=sem, _incn=16, out=out_ap, in_=in_ap)

    def mm_group(bank, out_ap, mms, pre=()):
        p.wait("tensor", bank.r)
        for m in pre:
            p.wait("tensor", m)
        mk = None
        n = len(mms)
        for i, (l, r) in enumerate(mms):
            mk = p.emit("tensor", "matmul", out_ap, l, r,
                        start=(i == 0), stop=(i == n - 1),
                        _inc=("mm" if i == n - 1 else None))
        bank.w = mk
        return mk

    dbg_tensors = []

    def dbg_dump(pairs):
        for eng_sem in semnames:
            if eng_sem != "dbg":
                p.wait("sync", p.marker(eng_sem))
        for name, ap in pairs:
            shp = [ap.shape[0], ap.shape[1]]
            t = nc.dram_tensor("dbg_" + name, shp, ap.tensor.dtype, kind="ExternalOutput").ap()
            dbg_tensors.append(name)
            dma("sync", t, ap, "dbg")
        p.wait("sync", p.marker("dbg"))

    def mm_single(bank, out_ap, l, r, pre=()):
        p.wait("tensor", bank.r)
        for m in pre:
            p.wait("tensor", m)
        mk = p.emit("tensor", "matmul", out_ap, l, r, start=True, stop=True, _inc="mm")
        bank.w = mk
        return mk

    cm = {}
    for ec in range(NEC):
        cm["tm%d" % ec] = dma("sync", fv(C_TM + ec, 1),
                              g["tmc"][ec:ec + 1, :].rearrange("a p -> p a"), "cn2")
    m_tm = p.marker("cn2")
    state0_loaded = []

    def init_rest():
        for ec in range(NEC):
            dma("sync", fv(C_TD + ec, 1), g["tdc"][ec:ec + 1, :].rearrange("a p -> p a"), "cnp")
        cm["cjc"] = dma("sync", fv(C_CJC, 1), g["cjc"], "cnp")
        cm["tdrow"] = dma("sync", fv(C_TDROW, D, 0, 1), g["tdrow"], "cnp")
        cm["cra"] = dma("sync", fv(C_CRA, 128), g["cra"], "cnp")
        cm["crbh"] = dma("sync", fv(C_CRBH, 128), g["crbh"], "cnp")
        cm["crq"] = dma("sync", fv(C_CRQ, 128), g["crq"], "cnp")
        cm["crbl"] = dma("sync", fv(C_CRBL, 64), g["crbl"], "cnp")
        cm["tril"] = dma("sync", fv(C_TRIL, 64, 0, 64), g["ctril"], "cn")
        cm["trih"] = dma("sync", fv(C_TRIH, 64), g["ctrih"], "cn")
        cm["trili"] = dma("sync", MSK[0:64, 0:64], g["ctrili"], "cn")
        cm["trihi"] = dma("sync", MSK[:, 64:128], g["ctrihi"], "cn")
        cm["trif"] = dma("sync", MSK[:, 128:256], g["ctrif"], "cn")
        cm["r16"] = dma("sync", fv(C_R16, 16), g["cr16"], "cn")
        cm["gr4"] = dma("sync", fv(C_GR4, 4), g["cgr4"], "cn")
        cm["idt"] = dma("sync", fv(C_IDT, 128), g["cidt"], "cn")
        cm["gw"] = dma("sync", fv(C_GW, 128).rearrange("p (dc e) -> p dc e", dc=8),
                       g["gwT"].rearrange("(dc p) e -> p dc e", p=128), "cn")
        cm["gb"] = dma("sync", fv(C_GBROW, 16, 0, 1), g["gbrow"], "cn")
        for ec in range(NEC):
            dma("sync", fv(C_TF + ec, 1), g["tfc"][ec:ec + 1, :].rearrange("a p -> p a"), "cn")

    wmark = {}
    for nm, off, sem in (("wkT", B_WK, "wk"), ("wvT", B_WV, "wv"), ("wrT", B_WR, "wr"),
                         ("wgT", B_WG, "wg"), ("woT", B_WO, "wo")):
        wmark[nm] = dma("gpsimd", bv(off, 8192).rearrange("p (dc e) -> p dc e", dc=8),
                        g[nm].rearrange("(dc p) e -> p dc e", p=128), sem)

    p.emit("vector", "memset", fv(C_ONESC, 1), 1.0)
    p.emit("vector", "memset", fv(C_ONESR, 128, 0, 1), 1.0)
    p.emit("vector", "memset", fv(C_EPSC, 1), 1e-5)
    p.emit("vector", "memset", fv(C_NEG1, 1), -1.0)
    p.emit("vector", "memset", fv(C_Z64, 128), 0.0)
    p.emit("vector", "memset", fv(A_S, 512), 0.0)
    p.emit("vector", "memset", bv(B_KT, 2048), 0.0)
    p.emit("vector", "memset", bv(B_SB, 512), 0.0)
    m_memset = p.emit("vector", "memset", bv(B_ATM, 256), 0.0, _inc="dv")
    if STOP == "pow":
        dbg_dump([("pa", fv(A_PA, 1024)), ("pbh", fv(A_PBH, 1024)), ("pq", fv(A_PQ, 1024)),
                  ("pbl", fv(A_PBL, 512)), ("pdt", fv(A_PDT, 1024)), ("wcc", fv(C_WCC, 8)),
                  ("td", fv(C_TD, 8)), ("tm", fv(C_TM, 8)), ("tf", fv(C_TF, 8))])

    STG_OFF = [A_STG, 16384]
    STGv = [fv(A_STG, 2048), fv(16384, 2048)]
    STG3v = [t.rearrange("p (dc t) -> p dc t", dc=8) for t in STGv]
    XXBv = [None, None]
    XXBv[0] = bv(B_XXB, 2048)
    XXBv[1] = bv(B_XXB2, 2048)
    xxb_rd2 = [Res(), Res()]
    vtok = [Res(), Res()]
    khat = [Res(), Res()]
    stg_rd = []
    xxb_rd = Res()
    m_chunk_mms = {}
    sb_pending = []
    m_stt = [None] * NEC
    pair_last = [None] * NEC
    m_supd = [None] * NEC
    m_wkv_pair = {}
    atm_res = [Res(), Res()]
    state = {"aob": None, "stats": None, "spill": None, "proj": {}}

    def scan_chunk(c, prefix):
        par = c % 2
        lc = par
        m_scal = {}
        if not prefix:
            for hp in range(NEC):
                KF = fv(A_KFEAT + hp * SCW + lc * 128, 128)
                if pair_last[hp] is not None:
                    p.wait("gpsimd", pair_last[hp])
                p.wait("gpsimd", state["proj"][hp])
                p.emit("gpsimd", "tensor_tensor", bv(B_APAIR + hp * 128, 128), KF,
                       fv(A_PA + hp * 128, 128), ALU.mult, _inc="gp")
                p.emit("gpsimd", "tensor_tensor", bv(B_BL + hp * 64, 64), KF[:, 0:64],
                       fv(A_PBL + hp * 64, 64), ALU.mult, _inc="gp")
                p.emit("gpsimd", "tensor_tensor", bv(B_BH + hp * 128, 128), KF,
                       fv(A_PBH + hp * 128, 128), ALU.mult, _inc="gp")
                last = None
                for h in (2 * hp, 2 * hp + 1):
                    rw = (h % 2) * 64
                    last = p.emit("gpsimd", "tensor_tensor",
                                  bv(B_KT + h * 128, 128, rw, 64), KF[rw:rw + 64, :],
                                  fv(A_PQ + hp * 128, 128, rw, 64), ALU.mult, _inc="gp")
                m_scal[hp] = last
                pool_rd[hp] = last
        last_mm = None
        for hp in range(NEC):
            heads = (2 * hp, 2 * hp + 1)
            if not prefix:
                for h in heads:
                    rw = (h % 2) * 64
                    ba = banks[6 + (h % 2)]
                    p.wait("tensor", ba.r)
                    p.wait("tensor", m_scal[hp])
                    p.emit("tensor", "matmul", ba.ap[0:64, 0:64],
                           bv(B_BL + hp * 64, 64, rw, 64),
                           bv(B_APAIR + hp * 128, 64, rw, 64), start=True, stop=True)
                    mat = p.emit("tensor", "matmul", ba.ap[:, 64:128],
                                 bv(B_BH + hp * 128, 128, rw, 64),
                                 bv(B_APAIR + hp * 128 + 64, 64, rw, 64),
                                 start=True, stop=True, _inc="mm")
                    ba.w = mat
                    atm = atm_res[h % 2]
                    p.wait("vector", mat)
                    if atm.r is not None:
                        p.wait("vector", atm.r)
                    ao = B_ATM + (h % 2) * 128
                    mev = p.emit("vector", "select", bv(ao, 128), MSK[:, 128:256],
                                 ba.ap[:, 0:128], fv(C_Z64, 128), _inc="dv")
                    ba.r = mev
                    atm.w = mev
                bw = banks[hp % 2]
                p.wait("tensor", bw.r)
                p.wait("tensor", khat[par].w)
                p.wait("tensor", vtok[par].w)
                if m_supd[hp] is not None:
                    p.wait("tensor", m_supd[hp])
                mwk = None
                for h in heads:
                    rw = (h % 2) * 64
                    atm = atm_res[h % 2]
                    p.wait("tensor", atm.w)
                    p.emit("tensor", "matmul", bw.ap[rw:rw + 64, 0:128],
                           bv(B_VTOK + par * 1024 + h * 64, 64), bv(B_ATM + (h % 2) * 128, 128),
                           start=True, stop=False)
                    mwk = p.emit("tensor", "matmul", bw.ap[rw:rw + 64, 0:128],
                                 bv(B_SB + hp * 64, 64), bv(B_KT + h * 128, 128),
                                 start=False, stop=True, _inc="mm")
                    atm.r = mwk
                m_wkv_pair[hp] = mwk
                pair_last[hp] = mwk
                bw.w = mwk
                KF = fv(A_KFEAT + hp * SCW + lc * 128, 128)
                VF = bv(B_VFEAT + hp * SCW + lc * 128, 128)
                p.wait("vector", mwk)
                p.emit("vector", "scalar_tensor_tensor", TF1, KF, fv(C_TF + hp, 1), VF,
                       ALU.mult, ALU.mult, _inc="dv")
                p.emit("vector", "tensor_tensor", TF2, bw.ap[:, 0:128], TF1, ALU.add, _inc="dv")
                mao = p.emit("vector", "tensor_tensor", bv(B_AOB + hp * SCW + lc * 128, 128),
                             TF2, fv(A_RG + hp * SCW + lc * 128, 128), ALU.mult, _inc="dv")
                bw.r = mao
                state["aob"] = mao
                scan_rd[hp] = mao
            bs = banks[6 + hp % 2]
            p.wait("tensor", bs.r)
            p.wait("tensor", khat[par].w)
            p.wait("tensor", vtok[par].w)
            mst = None
            for h in heads:
                rw = (h % 2) * 64
                mst = p.emit("tensor", "matmul", bs.ap[rw:rw + 64, 0:64],
                             bv(B_KHAT + par * 1024 + h * 64, 64),
                             bv(B_VTOK + par * 1024 + h * 64, 64),
                             start=True, stop=True, _inc="mm")
            bs.w = mst
            last_mm = mst
            if not prefix:
                p.wait("vector", m_wkv_pair[hp])
            p.wait("vector", mst)
            mstt = p.emit("vector", "scalar_tensor_tensor", fv(A_S + hp * 64, 64),
                          fv(A_S + hp * 64, 64), fv(C_WCC + hp, 1), bs.ap[:, 0:64],
                          ALU.mult, ALU.add, _inc="dv")
            bs.r = mstt
            m_stt[hp] = mstt
            sb_pending.append(hp)
        for hp in sb_pending:
            p.wait("gpsimd", m_stt[hp])
            msu = p.emit("gpsimd", "tensor_copy", bv(B_SB + hp * 64, 64),
                         fv(A_S + hp * 64, 64), _inc="gp")
            m_supd[hp] = msu
        sb_pending.clear()
        m_chunk_mms[c] = last_mm
        vtok[par].r = last_mm
        khat[par].r = last_mm

    spill_mark = [None, None]
    stg_rd2 = [[], []]
    scan_rd = [None] * NEC
    pool_rd = [None] * NEC

    def st_load(s, prefix):
        slot = s % 2
        STG = STGv[slot]
        STG3 = STG3v[slot]
        xsrc = g["xTpre"] if prefix else g["xT"]
        for m in stg_rd2[slot]:
            p.wait("sync", m)
        stg_rd2[slot] = []
        p.wait("sync", spill_mark[slot])
        mx = dma("sync", STG3,
                 xsrc.rearrange("(dc p) t -> p dc t", p=128)[:, :, s * SCW:(s + 1) * SCW],
                 "sg%d" % slot)
        XXB = XXBv[slot]
        p.wait("scalar", mx)
        p.wait("scalar", m_tm)
        p.wait("scalar", xxb_rd2[slot].r)
        mxx = None
        for dc in range(NEC):
            mxx = p.emit("scalar", "activation", XXB[:, dc * SCW:(dc + 1) * SCW],
                         STG[:, dc * SCW:(dc + 1) * SCW], AF.Copy,
                         scale=fv(C_TM + dc, 1), _inc="ac")
        if prefix:
            stg_rd2[slot].append(mxx)
        state["xxb_w"] = mxx
        state["xxb"] = XXB

    def st_tok(s, base, prefix):
        xxb_w = state["xxb_w"]
        XXB = state["xxb"]
        last_tok = None
        for tt in range(2):
            c = base + 2 * s + tt
            par = c % 2
            prev = m_chunk_mms.get(c - 2)
            for hf in range(2):
                for pj in ("k", "v"):
                    bk = banks[6 + (tt * 2 + hf) % 2]
                    woff = B_WK if pj == "k" else B_WV
                    mms = [(XXB[:, dc * SCW + tt * 128: dc * SCW + tt * 128 + 128],
                            bv(woff + dc * 1024 + hf * 512, 512)) for dc in range(NEC)]
                    mk = mm_group(bk, bk.ap[:, 0:512], mms,
                                  pre=(xxb_w, wmark["wkT" if pj == "k" else "wvT"]))
                    last_tok = mk
                    if not prefix and s == NSC - 1:
                        state["wk_dead" if pj == "k" else "wv_dead"] = mk
                    if pj == "k":
                        p.wait("vector", mk)
                        if prev is not None:
                            p.wait("vector", prev)
                        mkh = p.emit("vector", "tensor_tensor",
                                     bv(B_KHAT + par * 1024 + hf * 512, 512),
                                     bk.ap[:, 0:512], fv(A_PDT + hf * 512, 512), ALU.mult,
                                     _inc="dv")
                        bk.r = mkh
                        khat[par].w = mkh
                    else:
                        p.wait("scalar", mk)
                        if prev is not None:
                            p.wait("scalar", prev)
                        mv = p.emit("scalar", "activation", bv(B_VTOK + par * 1024 + hf * 512, 512),
                                    bk.ap[:, 0:512], AF.Copy, _inc="ac")
                        bk.r = mv
                        vtok[par].w = mv
        xxb_rd2[s % 2].r = last_tok

    def st_feat(s):
        xxb_w = state["xxb_w"]
        XXB = state["xxb"]
        for ec in range(NEC):
            bkg = banks[2 + (ec * 4) % 4]
            bkr = banks[2 + (ec * 4 + 1) % 4]
            bkk = banks[2 + (ec * 4 + 2) % 4]
            bkv = banks[2 + (ec * 4 + 3) % 4]

            def fmm(woff):
                return [(bv(woff + dc * 1024 + ec * 128, 128), XXB[:, dc * SCW:(dc + 1) * SCW])
                        for dc in range(NEC)]

            mg = mm_group(bkg, bkg.ap[:, 0:SCW], fmm(B_WG), pre=(xxb_w, wmark["wgT"]))
            mr = mm_group(bkr, bkr.ap[:, 0:SCW], fmm(B_WR), pre=(wmark["wrT"],))
            mk_ = mm_group(bkk, bkk.ap[:, 0:SCW], fmm(B_WK), pre=())
            mv_ = mm_group(bkv, bkv.ap[:, 0:SCW], fmm(B_WV), pre=())
            p.wait("scalar", scan_rd[ec])
            p.wait("scalar", pool_rd[ec])
            p.wait("scalar", mg)
            msil = p.emit("scalar", "activation", fv(A_RG + ec * SCW, SCW), bkg.ap[:, 0:SCW],
                          AF.Silu, _inc="ac")
            bkg.r = msil
            p.wait("vector", mr)
            p.wait("vector", msil)
            mrg = p.emit("vector", "tensor_tensor", fv(A_RG + ec * SCW, SCW), bkr.ap[:, 0:SCW],
                         fv(A_RG + ec * SCW, SCW), ALU.mult, _inc="dv")
            bkr.r = mrg
            p.wait("scalar", mk_)
            mkf = p.emit("scalar", "activation", fv(A_KFEAT + ec * SCW, SCW), bkk.ap[:, 0:SCW],
                         AF.Copy, _inc="ac")
            bkk.r = mkf
            p.wait("scalar", mv_)
            mvf = p.emit("scalar", "activation", bv(B_VFEAT + ec * SCW, SCW), bkv.ap[:, 0:SCW],
                         AF.Copy, _inc="ac")
            bkv.r = mvf
            state["proj"][ec] = mkf
            state["proj_extra", ec] = (mrg, mvf)

    def st_scan(s, base, prefix):
        if not prefix:
            for hp in range(NEC):
                mrg, mvf = state["proj_extra", hp]
                p.wait("vector", mvf)
        for tt in range(2):
            scan_chunk(base + 2 * s + tt, prefix)

    def st_tail(s):
        slot = s % 2
        STG = STGv[slot]
        STG3 = STG3v[slot]
        m_z1 = {}
        for ec in range(NEC):
            bk = banks[4 + ec % 2]
            mms = [(bv(B_WO + dc * 1024 + ec * 128, 128), bv(B_AOB + dc * SCW, SCW))
                   for dc in range(NEC)]
            mk = mm_group(bk, bk.ap[:, 0:SCW], mms, pre=(state["aob"], wmark["woT"]))
            p.wait("vector", mk)
            mz = p.emit("vector", "tensor_tensor", STG[:, ec * SCW:(ec + 1) * SCW],
                        bk.ap[:, 0:SCW], STG[:, ec * SCW:(ec + 1) * SCW], ALU.add, _inc="dv")
            bk.r = mz
            m_z1[ec] = mz
        bk1, bk2 = banks[2], banks[3]
        p.wait("tensor", bk1.r)
        p.wait("tensor", bk2.r)
        SQ1 = fv(A_TF, SCW)
        SQ2 = fv(A_TF + SCW, SCW)
        last1 = last2 = None
        sqmm = [None, None]
        for dc in range(NEC):
            p.wait("vector", m_z1[dc])
            sq = SQ1 if dc % 2 == 0 else SQ2
            p.wait("vector", sqmm[dc % 2])
            msq = p.emit("vector", "tensor_tensor", sq, STG[:, dc * SCW:(dc + 1) * SCW],
                         STG[:, dc * SCW:(dc + 1) * SCW], ALU.mult, _inc="dv")
            p.wait("tensor", m_z1[dc])
            last1 = p.emit("tensor", "matmul", bk1.ap[0:1, 0:SCW], fv(C_ONESC, 1),
                           STG[:, dc * SCW:(dc + 1) * SCW], start=(dc == 0),
                           stop=(dc == NEC - 1), _inc=("mm" if dc == NEC - 1 else None))
            p.wait("tensor", msq)
            last2 = p.emit("tensor", "matmul", bk2.ap[0:1, 0:SCW], fv(C_ONESC, 1), sq,
                           start=(dc == 0), stop=(dc == NEC - 1), _inc="mm")
            sqmm[dc % 2] = ("mm", p.cnt["mm"])
        bk1.w, bk2.w = last1, last2
        p.wait("vector", last1)
        mrow = p.emit("vector", "tensor_scalar", fv(A_MROW + s * SCW, SCW, 0, 1),
                      bk1.ap[0:1, 0:SCW], 1.0 / 1024.0, None, ALU.mult, _inc="dv")
        bk1.r = mrow
        p.wait("vector", mrow)
        msqr = p.emit("vector", "tensor_tensor", fv(A_MSQ, SCW, 0, 1),
                      fv(A_MROW + s * SCW, SCW, 0, 1), fv(A_MROW + s * SCW, SCW, 0, 1),
                      ALU.mult, _inc="dv")
        p.wait("vector", last2)
        p.wait("vector", msqr)
        mvar = p.emit("vector", "scalar_tensor_tensor", fv(A_VROW + s * SCW, SCW, 0, 1),
                      bk2.ap[0:1, 0:SCW], 1.0 / 1024.0, fv(A_MSQ, SCW, 0, 1),
                      ALU.mult, ALU.subtract, _inc="dv")
        bk2.r = mvar
        state["stats"] = (last2, mvar)
        mkl0 = None
        mkv_last = None
        for tt in range(2):
            gtt = s * 2 + tt
            bk = banks[4 + tt]
            mms = [(STG[:, dc * SCW + tt * 128: dc * SCW + tt * 128 + 128],
                    fv(C_GW + dc * 16, 16)) for dc in range(NEC)]
            mkl0 = mm_group(bk, bk.ap[:, 0:16], mms, pre=(m_z1[NEC - 1], m_consts))
            p.wait("vector", mkl0)
            ml0 = p.emit("vector", "tensor_copy", fv(C_SCORES + gtt * 16, 16),
                         bk.ap[:, 0:16], _inc="dv")
            bk.r = ml0
            bkc = banks[2 + tt]
            mkc = mm_single(bkc, bkc.ap[:, 0:1],
                            FB[0:1, A_MROW + s * SCW + tt * 128: A_MROW + s * SCW + tt * 128 + 128],
                            fv(C_NEG1, 1, 0, 1), pre=(mrow,))
            p.wait("vector", mkc)
            mmc = p.emit("vector", "tensor_copy", fv(C_MC + gtt, 1), bkc.ap[:, 0:1], _inc="dv")
            bkc.r = mmc
            mkv = mm_single(bkc, bkc.ap[:, 1:2],
                            FB[0:1, A_VROW + s * SCW + tt * 128: A_VROW + s * SCW + tt * 128 + 128],
                            fv(C_ONESC, 1, 0, 1), pre=(mvar,))
            mkv_last = mkv
            p.wait("vector", mkv)
            mvc = p.emit("vector", "tensor_copy", fv(C_VC + gtt, 1), bkc.ap[:, 1:2], _inc="dv")
            bkc.r = mvc
            state["mvc"] = mvc
        VRS = fv(A_VROW + s * SCW, SCW, 0, 1)
        p.wait("scalar", mkv_last)
        p.wait("scalar", mvar)
        msd1 = p.emit("scalar", "activation", VRS, VRS, AF.Sqrt,
                      bias=fv(C_EPSC, 1, 0, 1), _inc="ac")
        p.wait("vector", msd1)
        mrs1 = p.emit("vector", "reciprocal", VRS, VRS, _inc="dv")
        bkmb, bkrs = banks[2], banks[3]
        mkmb = mm_single(bkmb, bkmb.ap[:, 256:512], fv(C_ONESR, 128, 0, 1),
                         fv(A_MROW + s * SCW, SCW, 0, 1), pre=(mrow,))
        mkrs = mm_single(bkrs, bkrs.ap[:, 256:512], fv(C_ONESR, 128, 0, 1), VRS,
                         pre=(mrs1,))
        p.wait("vector", mkmb)
        p.wait("vector", mkrs)
        p.wait("vector", last2)
        p.wait("vector", mkl0)
        mx1l = None
        for ec in range(NEC):
            p.emit("vector", "tensor_tensor", STG[:, ec * SCW:(ec + 1) * SCW],
                   STG[:, ec * SCW:(ec + 1) * SCW], bkmb.ap[:, 256:512], ALU.subtract,
                   _inc="dv")
            mx1l = p.emit("vector", "tensor_tensor", STG[:, ec * SCW:(ec + 1) * SCW],
                          STG[:, ec * SCW:(ec + 1) * SCW], bkrs.ap[:, 256:512], ALU.mult,
                          _inc="dv")
        bkmb.r = mx1l
        bkrs.r = mx1l
        stg_rd2[slot].extend([last2, mvar, mkl0, mx1l])
        state["mx1l", s] = mx1l
        p.wait("sync", mx1l)
        msp = dma("sync",
                  x1sp.rearrange("(dc p) t -> p dc t", p=128)[:, :, s * SCW:(s + 1) * SCW],
                  STG3, "sp%d" % slot)
        if s < NSC - 2:
            p.wait("gpsimd", mx1l)
            mspb = dma("gpsimd",
                       x1bsp.rearrange("(dc p) t -> p dc t", p=128)[:, :, s * SCW:(s + 1) * SCW],
                       STG3, "rl%d" % s)
            state["spill_b", s] = mspb
            stg_rd2[slot].append(("rl%d" % s, 16))
        spill_mark[slot] = msp
        state["spill"] = msp

    _EARLY = ("prefix", "sc0", "attn", "xx", "tok", "feat", "scan")
    st_load(0, True)
    init_rest()
    m_cnp = p.marker("cnp")
    m_consts = p.marker("cn")

    TF1 = fv(A_TF, 128)
    TF2 = fv(A_TF + 128, 128)
    p.wait("scalar", m_cnp)
    for ec in range(NEC):
        tdcol = fv(C_TD + ec, 1)
        for src, dst, w in ((C_CRA, A_PA + ec * 128, 128), (C_CRBH, A_PBH + ec * 128, 128),
                            (C_CRQ, A_PQ + ec * 128, 128), (C_CRBL, A_PBL + ec * 64, 64)):
            p.emit("scalar", "activation", fv(dst, w), fv(src, w), AF.Exp, scale=tdcol, _inc="ac")
    m_wcc = p.emit("scalar", "activation", fv(C_WCC, 8), fv(C_TD, 8), AF.Exp, scale=-128.0, _inc="ac")
    m_pdt = None
    for hf in range(2):
        bk = banks[4 + hf]
        mk = mm_single(bk, pb[4 + hf][:, 0:512], fv(C_ONESR, 128, 0, 1),
                       fv(C_TDROW + hf * 512, 512, 0, 1), pre=(m_cnp, m_memset))
        p.wait("scalar", mk)
        m_pdt = p.emit("scalar", "activation", fv(A_PDT + hf * 512, 512), pb[4 + hf][:, 0:512],
                       AF.Exp, scale=fv(C_CJC, 1), _inc="ac")
        bk.r = m_pdt
    p.wait("vector", m_pdt)
    for s in range(NSC):
        if s > 0:
            st_load(s, True)
        st_tok(s, 0, True)
        st_scan(s, 0, True)
    if STOP == "prefix":
        dbg_dump([("s", fv(A_S, 512)), ("sb", bv(B_SB, 512)),
                  ("khat", bv(B_KHAT, 2048)), ("vtok", bv(B_VTOK, 2048)),
                  ("xxb", bv(B_XXB, 2048))])
    else:
        nmain = 1 if STOP in ("sc0", "xx", "tok", "feat", "scan") else NSC
        p.wait("vector", m_consts)
        p.wait("scalar", m_consts)
        p.wait("tensor", m_consts)
        bkg2 = banks[4]
        p.wait("tensor", bkg2.r)
        p.wait("tensor", m_consts)
        lastg = None
        for dc in range(NEC):
            lastg = p.emit("tensor", "matmul", bkg2.ap[0:1, 0:16], fv(C_ONESC, 1),
                           fv(C_GW + dc * 16, 16), start=(dc == 0), stop=(dc == NEC - 1),
                           _inc=("mm" if dc == NEC - 1 else None))
        bkg2.w = lastg
        p.wait("vector", lastg)
        mgrow = p.emit("vector", "tensor_copy", fv(C_GROW, 16, 0, 1), bkg2.ap[0:1, 0:16],
                       _inc="dv")
        bkg2.r = mgrow
        p.wait("tensor", mgrow)
        mkb = mm_single(bkg2, bkg2.ap[:, 0:16], fv(C_ONESR, 128, 0, 1), fv(C_GROW, 16, 0, 1))
        p.wait("vector", mkb)
        mgb2 = p.emit("vector", "tensor_copy", fv(C_GB2, 16), bkg2.ap[:, 0:16], _inc="dv")
        bkg2.r = mgb2
        st_load(0, False)
        st_feat(0)
        st_tok(0, NSC * 2, False)
        for s in range(nmain):
            st_scan(s, NSC * 2, False)
            if s + 1 < NSC and STOP not in ("xx", "tok", "feat", "scan"):
                st_load(s + 1, False)
                st_feat(s + 1)
            st_tail(s)
            if s + 1 < NSC and STOP not in ("xx", "tok", "feat", "scan"):
                st_tok(s + 1, NSC * 2, False)
            if s == NSC - 2 and STOP not in ("xx", "tok", "feat", "scan"):
                x1br3 = x1bsp.rearrange("(dc p) t -> p dc t", p=128)
                nready = NSC - 2
                for dc in range(NEC):
                    p.wait("sync", state["wk_dead" if dc < 4 else "wv_dead"])
                    for sp_ in range(nready):
                        p.wait("sync", state["spill_b", sp_])
                    dma("sync", bv(B_X1B + dc * 2048, nready * SCW),
                        x1br3[:, dc, 0:nready * SCW], "xr%d" % dc)
            if STOP in ("xx", "tok", "feat", "scan"):
                break
        attn_done_mm = p.marker("mm")
        if STOP == "sc0":
            dbg_dump([("kfeat", fv(A_KFEAT, 2048)), ("rg", fv(A_RG, 2048)),
                      ("vfeat", bv(B_VFEAT, 2048)), ("xxb", bv(B_XXB, 2048)),
                      ("aob", bv(B_AOB, 2048)), ("s", fv(A_S, 512)),
                      ("z1", fv(A_STG, 2048)), ("apair", bv(B_APAIR, 1024)),
                      ("atm", bv(B_ATM, 256)), ("khat", bv(B_KHAT, 2048)),
                      ("vtok", bv(B_VTOK, 2048)), ("mrow", fv(A_MROW, 2048, 0, 1)),
                      ("vrow", fv(A_VROW, 2048, 0, 1))])
        elif STOP == "attn":
            dbg_dump([("mrow", fv(A_MROW, 2048, 0, 1)), ("vrow", fv(A_VROW, 2048, 0, 1)),
                      ("mc", fv(C_MC, 16)), ("vc", fv(C_VC, 16)),
                      ("scores", fv(C_SCORES, 256))])

    def _materialize_stop():
        def runner(engname):
            def run(eng):
                for item in p.q[engname]:
                    if item[0] == "w":
                        eng.wait_ge(sems[item[1]], item[2])
                    else:
                        _, fn, args, kwargs, inc, incn = item
                        r = getattr(eng, fn)(*args, **kwargs)
                        if inc:
                            r.then_inc(sems[inc], incn)
            return run
        with nc.Block() as block:
            block.sync(runner("sync"))
            block.gpsimd(runner("gpsimd"))
            block.scalar(runner("scalar"))
            block.vector(runner("vector"))
            block.tensor(runner("tensor"))
        ctx.close()

    if STOP in _EARLY or STOP == "pow":
        _materialize_stop()
        return nc
    p.wait("sync", m_chunk_mms[NSC * 4 - 1])
    p.wait("sync", state["stats"][0])
    m_x1b_cp_last = None
    for sp_ in (NSC - 2, NSC - 1):
        STGc = STGv[sp_ % 2]
        p.wait("scalar", state["mx1l", sp_])
        for ec in range(NEC):
            p.wait("scalar", state["wk_dead" if ec < 4 else "wv_dead"])
            m_x1b_cp_last = p.emit("scalar", "activation",
                                   bv(B_X1B + ec * 2048 + sp_ * SCW, SCW),
                                   STGc[:, ec * SCW:(ec + 1) * SCW], AF.Copy, _inc="ac")
    x1b_ready = [("xr%d" % dc, 16) for dc in range(NEC)] + [m_x1b_cp_last]
    bk = banks[0]
    mk = mm_single(bk, bk.ap[:, 0:16], fv(C_ONESR, 128, 0, 1), fv(C_GBROW, 16, 0, 1),
                   pre=(cm["gb"],))
    p.wait("vector", mk)
    mgbb = p.emit("vector", "tensor_copy", fv(C_GBB, 16), bk.ap[:, 0:16], _inc="dv")
    bk.r = mgbb
    m_cw = None
    p.wait("scalar", state["stats"][1])
    p.wait("scalar", state.get("mvc"))
    msdg = p.emit("scalar", "activation", fv(C_SD2, 16), fv(C_VC, 16), AF.Sqrt,
                  bias=fv(C_EPSC, 1), _inc="ac")
    p.wait("vector", msdg)
    mrsg = p.emit("vector", "reciprocal", fv(C_RSC, 16), fv(C_SD2, 16), _inc="dv")
    p.wait("vector", mrsg)
    msc_last = None
    for ttl in range(NTT):
        if msc_last is not None:
            p.wait("vector", msc_last)
        p.emit("vector", "scalar_tensor_tensor", fv(C_SD2, 16), fv(C_GB2, 16),
               fv(C_MC + ttl, 1), fv(C_SCORES + ttl * 16, 16), ALU.mult, ALU.add)
        p.fence("vector", "dv")
        mpre = p.emit("vector", "tensor_scalar", fv(C_SD2, 16), fv(C_SD2, 16),
                      fv(C_RSC + ttl, 1), None, ALU.mult, _inc="dv")
        p.wait("scalar", mpre)
        msc_last = p.emit("scalar", "activation", fv(C_SCORES + ttl * 16, 16), fv(C_SD2, 16),
                          AF.Sigmoid, _inc="ac")
    m_gate_mm = p.marker("mm")

    m_cw_box = [None]
    gate_steps = []

    def gstep(f):
        gate_steps.append(f)

    def plan_gate_logic():
        def fence():
            p.fence("vector", "dv")

        def s_sfc():
            p.wait("vector", msc_last)
            p.wait("vector", mgbb)
            p.wait("vector", spill_mark[1])
            p.wait("vector", m_x1b_cp_last)
            for ttl in range(NTT):
                p.emit("vector", "tensor_tensor", fv(GT_SFC + ttl * 16, 16),
                       fv(C_SCORES + ttl * 16, 16), fv(C_GBB, 16), ALU.add)
            fence()
        gstep(s_sfc)
        first = [True]
        for i in range(4):
            for j in range(i + 1, 4):
                def s_pair(i=i, j=j):
                    tgt = GT_GS if first[0] else GT_GK
                    for ttl in range(NTT):
                        p.emit("vector", "tensor_tensor", fv(tgt + ttl * 4, 4),
                               FB[:, GT_SFC + ttl * 16 + i: GT_SFC + ttl * 16 + 16: 4],
                               FB[:, GT_SFC + ttl * 16 + j: GT_SFC + ttl * 16 + 16: 4], ALU.add)
                    fence()
                    if not first[0]:
                        for ttl in range(NTT):
                            p.emit("vector", "tensor_tensor", fv(GT_GS + ttl * 4, 4),
                                   fv(GT_GS + ttl * 4, 4), fv(GT_GK + ttl * 4, 4), ALU.max)
                        fence()
                    first[0] = False
                gstep(s_pair)
        def s_gkey():
            for ttl in range(NTT):
                p.emit("vector", "tensor_tensor", fv(GT_GK + ttl * 4, 4), fv(GT_GS + ttl * 4, 4),
                       fv(C_GR4, 4), ALU.subtract)
            fence()
        gstep(s_gkey)
        for gg in range(4):
            def s_cnt(gg=gg):
                for ttl in range(NTT):
                    if gg == 0:
                        p.emit("vector", "tensor_scalar", fv(GT_CNT + ttl * 4, 4),
                               fv(GT_GK + ttl * 4, 4), fv(GT_GK + ttl * 4, 1), None, ALU.is_lt)
                    else:
                        p.emit("vector", "scalar_tensor_tensor", fv(GT_CNT + ttl * 4, 4),
                               fv(GT_GK + ttl * 4, 4), fv(GT_GK + ttl * 4 + gg, 1),
                               fv(GT_CNT + ttl * 4, 4), ALU.is_lt, ALU.add)
                fence()
            gstep(s_cnt)
        def s_sel4():
            p.emit("vector", "tensor_scalar", fv(GT_SEL, 64), fv(GT_CNT, 64), 1.5, None, ALU.is_le)
            fence()
        gstep(s_sel4)
        for gg in range(4):
            def s_key(gg=gg):
                for ttl in range(NTT):
                    p.emit("vector", "tensor_scalar", fv(GT_KEY + ttl * 16 + 4 * gg, 4),
                           fv(GT_SFC + ttl * 16 + 4 * gg, 4), fv(GT_SEL + ttl * 4 + gg, 1),
                           None, ALU.mult)
                if gg == 3:
                    fence()
            gstep(s_key)
        def s_ramp():
            for ttl in range(NTT):
                p.emit("vector", "tensor_tensor", fv(GT_KEY + ttl * 16, 16),
                       fv(GT_KEY + ttl * 16, 16), fv(C_R16, 16), ALU.subtract)
            fence()
        gstep(s_ramp)
        for e in range(16):
            def s_rank(e=e):
                for ttl in range(NTT):
                    if e == 0:
                        p.emit("vector", "tensor_scalar", fv(GT_CWU + ttl * 16, 16),
                               fv(GT_KEY + ttl * 16, 16), fv(GT_KEY + ttl * 16, 1), None,
                               ALU.is_lt)
                    else:
                        p.emit("vector", "scalar_tensor_tensor", fv(GT_CWU + ttl * 16, 16),
                               fv(GT_KEY + ttl * 16, 16), fv(GT_KEY + ttl * 16 + e, 1),
                               fv(GT_CWU + ttl * 16, 16), ALU.is_lt, ALU.add)
                fence()
            gstep(s_rank)
        def s_tail():
            p.emit("vector", "tensor_scalar", fv(GT_CWU, 256), fv(GT_CWU, 256), 3.5, None,
                   ALU.is_le)
            p.fence("vector", "dv")
            p.emit("vector", "tensor_tensor", fv(GT_CWU, 256), fv(GT_CWU, 256),
                   fv(C_SCORES, 256), ALU.mult)
            p.fence("vector", "dv")
            for ttl in range(NTT):
                p.emit("vector", "reduce_sum", fv(GT_WS + ttl, 1), fv(GT_CWU + ttl * 16, 16),
                       axis=AX.X)
            p.fence("vector", "dv")
            p.emit("vector", "tensor_scalar", fv(GT_WS, 16), fv(GT_WS, 16), 1e-20, None, ALU.add)
            p.fence("vector", "dv")
            p.emit("vector", "reciprocal", fv(GT_WI, 16), fv(GT_WS, 16))
            p.fence("vector", "dv")
            m_cw = None
            for ttl in range(NTT):
                m_cw = p.emit("vector", "tensor_scalar", fv(C_CW + ttl * 16, 16),
                              fv(GT_CWU + ttl * 16, 16), fv(GT_WI + ttl, 1), None, ALU.mult,
                              _inc="dv")
            m_cw_box[0] = m_cw
        gstep(s_tail)
    p.wait("vector", m_gate_mm)
    if STOP == "gate":
        dbg_dump([("scores", fv(C_SCORES, 256)), ("mc", fv(C_MC, 16)),
                  ("vc", fv(C_VC, 16)), ("rsc", fv(C_RSC, 16)), ("gb2", fv(C_GB2, 16)),
                  ("mrow", fv(A_MROW, 2048, 0, 1)), ("vrow", fv(A_VROW, 2048, 0, 1)),
                  ("x1b", bv(B_X1B, 16384))])
        _materialize_stop()
        return nc

    p.wait("gpsimd", attn_done_mm)
    wgu_slot = [Res() for _ in range(6)]
    wd_slot = [Res() for _ in range(NIC)]
    silu_res = [Res(), Res()]
    st_e = {"m_down_last": None, "myd": None, "m_cw": None}

    def wsrc_gu(e, ic, which):
        if e < NEXP:
            t = g["gpT"] if which == 0 else g["upT"]
            return t[e][:, ic * 128:(ic + 1) * 128].rearrange("(dc p) i -> p dc i", p=128)
        hf = e - NEXP
        t = g["sgT"] if which == 0 else g["suT"]
        return t[:, hf * I_EXP + ic * 128: hf * I_EXP + (ic + 1) * 128].rearrange(
            "(dc p) i -> p dc i", p=128)

    def wsrc_d(e, ic):
        if e < NEXP:
            return g["dpT"][e][ic * 128:(ic + 1) * 128, :]
        hf = e - NEXP
        return g["sdT"][hf * I_EXP + ic * 128: hf * I_EXP + (ic + 1) * 128, :]

    dma_map = {}
    all_pairs = [(e, ic) for e in range(NPSEUDO) for ic in range(NIC)]

    def emit_gu_dma(pair_idx):
        e, ic = all_pairs[pair_idx]
        ms = []
        for which in range(2):
            sl_i = (pair_idx * 2 + which) % 6
            sl = wgu_slot[sl_i]
            p.wait("gpsimd", sl.r)
            ms.append((sl_i,
                       dma("gpsimd",
                           bv(B_WGU + sl_i * 1024, 1024).rearrange("p (dc i) -> p dc i", dc=8),
                           wsrc_gu(e, ic, which), "g%d" % sl_i)))
        dma_map[(e, ic)] = ms

    for i in range(3):
        emit_gu_dma(i)

    def plan_wd(e):
        wd_m = []
        for ic in range(NIC):
            p.wait("gpsimd", wd_slot[ic].r)
            wd_m.append(dma("gpsimd", bv(B_WD + ic * 1024, 1024), wsrc_d(e, ic), "d%d" % ic))
        return wd_m

    def plan_gu(e, steps=None):
        m_act_last = None
        for ic in range(NIC):
            if steps:
                for _ in range(4):
                    if steps:
                        steps.pop(0)()
            (slg, mg), (slu, mu) = dma_map.pop((e, ic))
            for half in range(2):
                t0, t1 = 2 * half, 2 * half + 1
                bset = 0 if (ic * 2 + half) % 2 == 0 else 4
                bg0, bg1 = banks[bset], banks[bset + 1]
                bu0, bu1 = banks[bset + 2], banks[bset + 3]
                for bk in (bg0, bg1, bu0, bu1):
                    p.wait("tensor", bk.r)
                p.wait("tensor", mg)
                p.wait("tensor", mu)
                for _xm in x1b_ready:
                    p.wait("tensor", _xm)
                for dc in range(NEC):
                    st, sp = (dc == 0), (dc == NEC - 1)
                    wgt = bv(B_WGU + slg * 1024 + dc * 128, 128)
                    wut = bv(B_WGU + slu * 1024 + dc * 128, 128)
                    x0 = bv(B_X1B + dc * 2048 + t0 * 512, 512)
                    x1v = bv(B_X1B + dc * 2048 + t1 * 512, 512)
                    p.emit("tensor", "matmul", bg0.ap[:, 0:512], wgt, x0, start=st, stop=sp,
                           _inc=("mm" if sp else None))
                    p.emit("tensor", "matmul", bg1.ap[:, 0:512], wgt, x1v, start=st, stop=sp,
                           _inc=("mm" if sp else None))
                    p.emit("tensor", "matmul", bu0.ap[:, 0:512], wut, x0, start=st, stop=sp,
                           _inc=("mm" if sp else None))
                    mker = p.emit("tensor", "matmul", bu1.ap[:, 0:512], wut, x1v, start=st,
                                  stop=sp, _inc=("mm" if sp else None))
                bg0.w = bg1.w = bu0.w = bu1.w = mker
                for bg, bu, tau in ((bg0, bu0, t0), (bg1, bu1, t1)):
                    sres = silu_res[tau % 2]
                    p.wait("scalar", bg.w)
                    if sres.r is not None:
                        p.wait("scalar", sres.r)
                    msil = p.emit("scalar", "activation", bv(B_SILU + (tau % 2) * 512, 512),
                                  bg.ap[:, 0:512], AF.Silu, _inc="ac")
                    bg.r = msil
                    sres.w = msil
                    p.wait("vector", bu.w)
                    p.wait("vector", msil)
                    if ic == 0 and half == 0 and tau == t0:
                        p.wait("vector", st_e["m_down_last"])
                    mact = p.emit("vector", "tensor_tensor",
                                  bv(B_ACT + ic * 2048 + tau * 512, 512),
                                  bu.ap[:, 0:512], bv(B_SILU + (tau % 2) * 512, 512), ALU.mult,
                                  _inc="dv")
                    bu.r = mact
                    sres.r = mact
                if half == 1:
                    wgu_slot[slg].r = mker
                    wgu_slot[slu].r = mker
                    if ic == NIC - 1:
                        m_act_last = mact
            nxt = e * NIC + ic + 3
            if nxt < len(all_pairs):
                emit_gu_dma(nxt)
        return m_act_last

    def plan_down(e, m_act_last, wd_m, tile_cb=None, first=False):
        myd = None
        for ttl in range(NTT):
            bset = 4 if ttl % 2 == 0 else 6
            b0, b1 = banks[bset], banks[bset + 1]
            p.wait("tensor", b0.r)
            p.wait("tensor", b1.r)
            p.wait("tensor", m_act_last)
            for _m in wd_m:
                p.wait("tensor", _m)
            p.wait("tensor", st_e["m_cw"])
            for ic in range(NIC):
                st, sp = (ic == 0), (ic == NIC - 1)
                a = bv(B_ACT + ic * 2048 + ttl * 128, 128)
                p.emit("tensor", "matmul", b0.ap[:, 0:512], a,
                       bv(B_WD + ic * 1024, 512), start=st, stop=sp,
                       _inc=("mm" if sp else None))
                mker = p.emit("tensor", "matmul", b1.ap[:, 0:512], a,
                              bv(B_WD + ic * 1024 + 512, 512), start=st, stop=sp,
                              _inc=("mm" if sp else None))
            b0.w = b1.w = mker
            for hh, bk in ((0, b0), (1, b1)):
                p.wait("vector", bk.w)
                if first and ttl == 0 and hh == 0:
                    p.wait("vector", spill_mark[0])
                    p.wait("vector", spill_mark[1])
                ysl = FB[:, ttl * 1024 + hh * 512: ttl * 1024 + hh * 512 + 512]
                if first:
                    myd = p.emit("vector", "tensor_scalar", ysl, bk.ap[:, 0:512],
                                 fv(C_CW + ttl * 16 + e, 1), None, ALU.mult, _inc="dv")
                elif e < NEXP:
                    myd = p.emit("vector", "scalar_tensor_tensor", ysl, bk.ap[:, 0:512],
                                 fv(C_CW + ttl * 16 + e, 1), ysl, ALU.mult, ALU.add, _inc="dv")
                else:
                    myd = p.emit("vector", "tensor_tensor", ysl, bk.ap[:, 0:512], ysl,
                                 ALU.add, _inc="dv")
                bk.r = myd
            if tile_cb is not None:
                tile_cb(ttl, myd)
        st_e["myd"] = myd
        st_e["m_down_last"] = p.marker("mm")
        for ic in range(NIC):
            wd_slot[ic].r = st_e["m_down_last"]

    plan_gate_logic()
    wd_m0 = plan_wd(0)
    m_act0 = plan_gu(0, gate_steps)
    while gate_steps:
        gate_steps.pop(0)()
    st_e["m_cw"] = m_cw_box[0]
    plan_down(0, m_act0, wd_m0, first=True)
    for e in range(1, NPSEUDO - 1):
        wd_m = plan_wd(e)
        m_act = plan_gu(e)
        plan_down(e, m_act, wd_m)

    xe_slot = [Res(), Res()]
    xe_slot[0].r = m_cw_box[0]
    xe_slot[1].r = m_cw_box[0]
    p.wait("sync", spill_mark[0])
    p.wait("sync", spill_mark[1])
    p.wait("sync", m_x1b_cp_last)
    outs_slot = [Res(), Res()]
    x1r = x1sp.rearrange("(dc p) t -> p dc t", p=128)

    L2C1S = C_SCORES
    L2C2S = C_SCORES + 16
    L2MS = C_SCORES + 32
    L2SDS = C_SCORES + 48
    end_state = {"sq": None, "mza": {}}

    def end_tile(ttl, m_ytile):
        sl = xe_slot[ttl % 2]
        p.wait("sync", sl.r)
        mxe = dma("sync", fv(E_XE + (ttl % 2) * 1024, 1024).rearrange("p (dc t) -> p dc t", dc=8),
                  x1r[:, :, ttl * 128:(ttl + 1) * 128], "xe%d" % (ttl % 2))
        mza = None
        mtr = None
        for dc in range(NEC):
            bk = banks[dc % 2]
            p.wait("tensor", bk.r)
            p.wait("tensor", mxe)
            p.wait("tensor", m_consts)
            mtr = p.emit("tensor", "transpose", bk.ap[:, 0:128],
                         fv(E_XE + (ttl % 2) * 1024 + dc * 128, 128), fv(C_IDT, 128), _inc="mm")
            bk.w = mtr
            ysl = FB[:, ttl * 1024 + dc * 128: ttl * 1024 + dc * 128 + 128]
            p.wait("vector", mtr)
            p.wait("vector", m_ytile)
            mza = p.emit("vector", "tensor_tensor", ysl, bk.ap[:, 0:128], ysl, ALU.add, _inc="dv")
            bk.r = mza
        sl.r = mtr
        YT = FB[:, ttl * 1024:(ttl + 1) * 1024]
        p.wait("scalar", mza)
        msq2 = p.emit("scalar", "activation", fv(E_SQ, 1024), YT, AF.Square,
                      accum_out=fv(L2C2S + ttl, 1), _inc="ac")
        p.emit("vector", "reduce_sum", fv(L2C1S + ttl, 1), YT, axis=AX.X, _inc="dv")
        end_state["sq"] = msq2
        end_state["mza"][ttl] = mza
        if ttl == 7:
            end_finish_range(0, 8)
        elif ttl == 15:
            end_finish_range(8, 16)

    def end_finish_range(lo, hi):
        n = hi - lo
        p.fence("vector", "dv")
        p.wait("vector", end_state["sq"])
        p.emit("vector", "tensor_scalar", fv(L2MS + lo, n), fv(L2C1S + lo, n), 1.0 / 1024.0,
               None, ALU.mult)
        p.fence("vector", "dv")
        p.emit("vector", "tensor_tensor", fv(L2SDS + lo, n), fv(L2MS + lo, n), fv(L2MS + lo, n),
               ALU.mult)
        p.fence("vector", "dv")
        mvar = p.emit("vector", "scalar_tensor_tensor", fv(L2SDS + lo, n), fv(L2C2S + lo, n),
                      1.0 / 1024.0, fv(L2SDS + lo, n), ALU.mult, ALU.subtract, _inc="dv")
        p.wait("scalar", mvar)
        msd = p.emit("scalar", "activation", fv(L2SDS + 16 + lo, n), fv(L2SDS + lo, n), AF.Sqrt,
                     bias=fv(C_EPSC, 1), _inc="ac")
        p.wait("vector", msd)
        p.emit("vector", "reciprocal", fv(L2SDS + 32 + lo, n), fv(L2SDS + 16 + lo, n))
        p.fence("vector", "dv")
        for ttl in range(lo, hi):
            YT = FB[:, ttl * 1024:(ttl + 1) * 1024]
            osl = outs_slot[ttl % 2]
            if osl.r is not None:
                p.wait("vector", osl.r)
            mo = p.emit("vector", "tensor_scalar", fv(E_OUTS + (ttl % 2) * 1024, 1024), YT,
                        fv(L2MS + ttl, 1), fv(L2SDS + 32 + ttl, 1), ALU.subtract, ALU.mult,
                        _inc="dv")
            p.wait("sync", mo)
            mst = dma("sync", out[ttl * 128:(ttl + 1) * 128, :],
                      fv(E_OUTS + (ttl % 2) * 1024, 1024), "ou%d" % (ttl % 2))
            osl.r = mst

    wd_m17 = plan_wd(NPSEUDO - 1)
    m_act17 = plan_gu(NPSEUDO - 1)
    plan_down(NPSEUDO - 1, m_act17, wd_m17, tile_cb=end_tile)
    if STOP == "fin":
        dbg_dump([("y", FB[:, 0:16384]), ("l2", fv(GS, 16)),
                  ("outs", fv(E_OUTS, 2048)), ("esq", fv(E_SQ, 1024)),
                  ("cw", fv(C_CW, 256))])
        _materialize_stop()
        return nc
    for snm in ["ou0", "ou1", "sp0", "sp1"] + ["rl%d" % i for i in range(NSC - 2)]:
        p.wait("sync", p.marker(snm))

    def runner(engname):
        def run(eng):
            for item in p.q[engname]:
                if item[0] == "w":
                    eng.wait_ge(sems[item[1]], item[2])
                else:
                    _, fn, args, kwargs, inc, incn = item
                    r = getattr(eng, fn)(*args, **kwargs)
                    if inc:
                        r.then_inc(sems[inc], incn)
        return run

    with nc.Block() as block:
        block.sync(runner("sync"))
        block.gpsimd(runner("gpsimd"))
        block.scalar(runner("scalar"))
        block.vector(runner("vector"))
        block.tensor(runner("tensor"))
    ctx.close()
    return nc


_NC_CACHE = None


def make_inmaps(inputs):
    f32 = np.float32
    x = np.asarray(inputs["L0_attn_x"], f32)
    tm = np.asarray(inputs["L0_attn_time_mix"], f32).reshape(-1)
    td = np.asarray(inputs["L0_attn_time_decay"], f32).reshape(-1)
    tf = np.asarray(inputs["L0_attn_time_first"], f32).reshape(-1)

    share = {}
    for nm, key in (("wkT", "L0_attn_Wk"), ("wvT", "L0_attn_Wv"), ("wrT", "L0_attn_Wr"),
                    ("wgT", "L0_attn_Wg"), ("woT", "L0_attn_Wo")):
        share[nm] = np.ascontiguousarray(np.asarray(inputs[key], f32).T)
    share["gwT"] = np.ascontiguousarray(np.asarray(inputs["L0_ffn_gate_w"], f32).T)
    share["gbrow"] = np.asarray(inputs["L0_ffn_gate_bias"], f32).reshape(1, 16)
    share["gpT"] = np.ascontiguousarray(np.asarray(inputs["L0_ffn_gate_proj"], f32).transpose(0, 2, 1))
    share["upT"] = np.ascontiguousarray(np.asarray(inputs["L0_ffn_up_proj"], f32).transpose(0, 2, 1))
    share["dpT"] = np.ascontiguousarray(np.asarray(inputs["L0_ffn_down_proj"], f32).transpose(0, 2, 1))
    share["sgT"] = np.ascontiguousarray(np.asarray(inputs["L0_ffn_shared_gate_w"], f32).T)
    share["suT"] = np.ascontiguousarray(np.asarray(inputs["L0_ffn_shared_up_w"], f32).T)
    share["sdT"] = np.ascontiguousarray(np.asarray(inputs["L0_ffn_shared_down_w"], f32).T)
    share["tdc"] = np.ascontiguousarray(td.reshape(NEC, 128))
    share["tfc"] = np.ascontiguousarray(tf.reshape(NEC, 128))
    share["tmc"] = np.ascontiguousarray(tm.reshape(NEC, 128))
    share["tdrow"] = td.reshape(1, D).copy()
    ii = np.arange(128, dtype=f32)
    cra = np.where(ii < 64, -(ii - 31.5), -(ii - 95.5)).astype(f32)
    share["cra"] = np.broadcast_to(cra, (128, 128)).copy()
    share["crbh"] = np.broadcast_to(-(95.5 - ii), (128, 128)).astype(f32).copy()
    share["crq"] = np.broadcast_to(-(ii + 1.0), (128, 128)).astype(f32).copy()
    share["crbl"] = np.broadcast_to(-(31.5 - ii[:64]), (128, 64)).astype(f32).copy()
    share["cjc"] = (-(127.0 - ii)).astype(f32).reshape(128, 1)
    jj = np.arange(64, dtype=f32)
    share["ctril"] = (jj[:, None] <= jj[None, :]).astype(f32)
    share["ctrih"] = (ii[:, None] <= (jj[None, :] + 64.0)).astype(f32)
    share["ctrili"] = (jj[:, None] <= jj[None, :]).astype(np.uint8)
    share["ctrihi"] = (ii[:, None] <= (jj[None, :] + 64.0)).astype(np.uint8)
    share["ctrif"] = (ii[:, None] <= ii[None, :]).astype(np.uint8)
    share["cr16"] = np.broadcast_to(np.arange(16, dtype=f32) * DELTA, (128, 16)).copy()
    share["cgr4"] = np.broadcast_to(np.arange(4, dtype=f32) * DELTA, (128, 4)).copy()
    share["cidt"] = np.eye(128, dtype=f32)

    zeros_pre = np.zeros((D, TLOC), f32)
    in_maps = []
    for core in range(8):
        b, half = core // 2, core % 2
        xTb = np.ascontiguousarray(x[b].T)
        m = dict(share)
        m["xT"] = np.ascontiguousarray(xTb[:, half * TLOC:(half + 1) * TLOC])
        m["xTpre"] = np.ascontiguousarray(xTb[:, 0:TLOC]) if half == 1 else zeros_pre
        in_maps.append(m)
    return in_maps


def kernel(**inputs):
    global _NC_CACHE
    f32 = np.float32
    in_maps = make_inmaps(inputs)
    if _NC_CACHE is None:
        _NC_CACHE = build_nc()
    res = run_bass_kernel_spmd(_NC_CACHE, in_maps, core_ids=list(range(8)))
    outp = np.empty((4, 4096, D), f32)
    if DEBUG:
        kernel.debug_results = res.results
    for core in range(8):
        b, half = core // 2, core % 2
        outp[b, half * TLOC:(half + 1) * TLOC, :] = np.asarray(res.results[core]["out"])
    return outp


# version 21
# speedup vs baseline: 1.0056x; 1.0056x over previous
import os
import numpy as np
from contextlib import ExitStack

import concourse.bass as bass
import concourse.mybir as mybir
from concourse.bass_utils import run_bass_kernel_spmd

F32 = mybir.dt.float32
BF16 = mybir.dt.bfloat16
AF = mybir.ActivationFunctionType
ALU = mybir.AluOpType
AX = mybir.AxisListType

DEBUG = bool(int(os.environ.get("MK_DEBUG", "0")))
STOP = os.environ.get("MK_STOP", "")

D = 1024
TLOC = 2048
NSC = 8
SCW = 256
NH = 16
HD = 64
NEC = 8
NEXP = 16
I_EXP = 1408
NIC = 11
NTT = 16
NPSEUDO = 18
DELTA = float(2.0 ** -40)

A_KFEAT = 0
A_RG = 2048
A_PA = 4096
A_PBH = 5120
A_PQ = 6144
A_PBL = 7168
A_PDT = 7680
A_STG = 8704
A_S = 10752
A_TF = 11264
A_MROW = 11776
A_VROW = 13824
A_MSQ = 15872
C_CRA = 11776
C_CRBH = 11904
C_CRQ = 12032
C_CRBL = 12160
C_TDROW = 12224
B_MB = 16384
B_RSB = 18432
E_XE = 16384
E_SQ = 18432
E_OUTS = 19456
C_SCORES = 21504
C_CW = 21760
C_GBB = 22016
C_GW = 22032
C_TD = 22160
C_TF = 22168
C_TM = 22176
C_WCC = 22184
C_ONESC = 22192
C_ONESR = 22193
C_EPSC = 22321
C_TRIL = 22322
C_TRIH = 22386
C_R16 = 22450
C_GR4 = 22466
C_IDT = 22470
C_CJC = 22598
C_GBROW = 22599
C_Z64 = 22615
GS = 22743
L2_C1 = GS + 0
L2_C2 = GS + 1
L2_M = GS + 2
L2_T = GS + 3
L2_V = GS + 4
L2_SD = GS + 5
C_CJC2 = GS + 6
C_MC = GS + 16
C_VC = GS + 32
C_RSC = GS + 48
C_SD2 = GS + 64
C_GB2 = GS + 80
C_GROW = GS + 96
C_NEG1 = GS + 112
F32_N = GS + 128
GT_SFC = 16384
GT_KEY = 16640
GT_GS = 16896
GT_GK = 16960
GT_CNT = 17024
GT_SEL = 17088
GT_CWU = 17152
GT_WS = 17408
GT_WI = 17424

B_WK = 0
B_WV = 8192
B_WR = 16384
B_WG = 24576
B_WO = 32768
B_XXB = 40960
B_VTOK = 43008
B_KHAT = 45056
B_VFEAT = 47104
B_BL = 49152
B_BH = 50176
B_KT = 52224
B_APAIR = 54272
B_ATM = 55296
B_AOB = 55552
ATTN_B16_END = B_AOB + 2048
B_X1B = 0
B_ACT = 16384
B_WGU = 38912
B_WD = 45056
B_SILU = 56320
B_SB = 57600
B_XXB2 = 58112
B16_N = 60160
assert ATTN_B16_END <= B16_N
assert F32_N * 4 + B16_N * 2 <= 207 * 1024, (F32_N * 4 + B16_N * 2)


class Planner:
    ENGINES = ("sync", "gpsimd", "scalar", "vector", "tensor")

    def __init__(self):
        self.q = {e: [] for e in self.ENGINES}
        self.cnt = {}
        self.waited = {e: {} for e in self.ENGINES}

    def wait(self, eng, marker):
        if marker is None:
            return
        s, v = marker
        if self.waited[eng].get(s, 0) >= v:
            return
        self.waited[eng][s] = v
        q = self.q[eng]
        if q and q[-1][0] == "w" and q[-1][1] == s:
            q[-1] = ("w", s, max(q[-1][2], v))
        else:
            q.append(("w", s, v))

    def emit(self, eng, fn, *args, _inc=None, _incn=1, **kwargs):
        self.q[eng].append(("i", fn, args, kwargs, _inc, _incn))
        if _inc:
            self.cnt[_inc] = self.cnt.get(_inc, 0) + _incn
            return (_inc, self.cnt[_inc])
        return None

    def marker(self, sem):
        return (sem, self.cnt.get(sem, 0))

    def fence(self, eng, sem):
        for idx in range(len(self.q[eng]) - 1, -1, -1):
            item = self.q[eng][idx]
            if item[0] == "i":
                if item[4] is None:
                    self.cnt[sem] = self.cnt.get(sem, 0) + 1
                    self.q[eng][idx] = (item[0], item[1], item[2], item[3], sem, 1)
                    mk = (sem, self.cnt[sem])
                else:
                    mk = (item[4], self.cnt[item[4]])
                self.wait(eng, mk)
                return mk
        return None


class Bank:
    __slots__ = ("ap", "w", "r")

    def __init__(self, ap):
        self.ap = ap
        self.w = None
        self.r = None


class Res:
    __slots__ = ("w", "r")

    def __init__(self):
        self.w = None
        self.r = None


def build_nc():
    nc = bass.Bass("TRN2", target_bir_lowering=False)
    g = {}

    def din(name, shape, dt=F32):
        g[name] = nc.dram_tensor(name, shape, dt, kind="ExternalInput").ap()

    din("xT", [D, TLOC]); din("xTpre", [D, TLOC])
    for w in ("wkT", "wvT", "wrT", "wgT", "woT"):
        din(w, [D, D])
    din("tdcT", [128, NEC]); din("tfcT", [128, NEC]); din("tmcT", [128, NEC]); din("tdrow", [1, D])
    din("gwT", [D, 16]); din("gbrow", [1, 16])
    din("gpT", [NEXP, D, I_EXP]); din("upT", [NEXP, D, I_EXP]); din("dpT", [NEXP, I_EXP, D])
    din("sgT", [D, 2816]); din("suT", [D, 2816]); din("sdT", [2816, D])
    din("crpack", [128, 448])
    din("cjc", [128, 1]); din("ctril", [64, 64]); din("ctrih", [128, 64])
    g["ctrili"] = nc.dram_tensor("ctrili", [64, 64], mybir.dt.uint8, kind="ExternalInput").ap()
    g["ctrihi"] = nc.dram_tensor("ctrihi", [128, 64], mybir.dt.uint8, kind="ExternalInput").ap()
    g["ctrif"] = nc.dram_tensor("ctrif", [128, 128], mybir.dt.uint8, kind="ExternalInput").ap()
    din("cr16", [128, 16]); din("cgr4", [128, 4]); din("cidt", [128, 128])
    out = nc.dram_tensor("out", [TLOC, D], F32, kind="ExternalOutput").ap()
    zkind = "ExternalOutput" if DEBUG else "Internal"
    x1bsp = nc.dram_tensor("x1bsp", [D, TLOC], BF16, kind=zkind).ap()
    x1sp = nc.dram_tensor("x1sp", [D, TLOC], F32, kind=zkind).ap()

    ctx = ExitStack()
    FB = ctx.enter_context(nc.sbuf_tensor("FB", [128, F32_N], F32))
    BB = ctx.enter_context(nc.sbuf_tensor("BB", [128, B16_N], BF16))
    pb = [ctx.enter_context(nc.psum_tensor("pb%d" % i, [128, 512], F32)) for i in range(8)]
    MSK = ctx.enter_context(nc.sbuf_tensor("MSK", [128, 256], mybir.dt.uint8))
    semnames = (["mm", "dv", "ac", "cn", "cn2", "cnp", "wk", "wv", "wr", "wg", "wo",
                 "sg0", "sg1", "sp0", "sp1", "xe0", "xe1",
                 "ou0", "ou1", "dbg", "gp"]
                + ["g%d" % i for i in range(6)] + ["d%d" % i for i in range(11)]
                + ["rl%d" % i for i in range(8)] + ["xr%d" % i for i in range(8)])
    sems = {s: ctx.enter_context(nc.semaphore(s)) for s in semnames}

    p = Planner()
    banks = [Bank(pb[i]) for i in range(8)]

    def fv(off, w, p0=0, pn=128):
        return FB[p0:p0 + pn, off:off + w]

    def bv(off, w, p0=0, pn=128):
        return BB[p0:p0 + pn, off:off + w]

    def dma(queue, out_ap, in_ap, sem):
        return p.emit(queue, "dma_start", _inc=sem, _incn=16, out=out_ap, in_=in_ap)

    def mm_group(bank, out_ap, mms, pre=()):
        p.wait("tensor", bank.r)
        for m in pre:
            p.wait("tensor", m)
        mk = None
        n = len(mms)
        for i, (l, r) in enumerate(mms):
            mk = p.emit("tensor", "matmul", out_ap, l, r,
                        start=(i == 0), stop=(i == n - 1),
                        _inc=("mm" if i == n - 1 else None))
        bank.w = mk
        return mk

    dbg_tensors = []

    def dbg_dump(pairs):
        for eng_sem in semnames:
            if eng_sem != "dbg":
                p.wait("sync", p.marker(eng_sem))
        for name, ap in pairs:
            shp = [ap.shape[0], ap.shape[1]]
            t = nc.dram_tensor("dbg_" + name, shp, ap.tensor.dtype, kind="ExternalOutput").ap()
            dbg_tensors.append(name)
            dma("sync", t, ap, "dbg")
        p.wait("sync", p.marker("dbg"))

    def mm_single(bank, out_ap, l, r, pre=()):
        p.wait("tensor", bank.r)
        for m in pre:
            p.wait("tensor", m)
        mk = p.emit("tensor", "matmul", out_ap, l, r, start=True, stop=True, _inc="mm")
        bank.w = mk
        return mk

    cm = {}
    cm["tm"] = dma("sync", fv(C_TM, NEC), g["tmcT"], "cn2")
    m_tm = p.marker("cn2")
    state0_loaded = []

    def init_rest():
        cm["tdrow"] = dma("sync", fv(C_TDROW, D, 0, 1), g["tdrow"], "cnp")
        dma("sync", fv(C_TD, NEC), g["tdcT"], "cnp")
        cm["cjc"] = dma("sync", fv(C_CJC, 1), g["cjc"], "cnp")
        cm["crpack"] = dma("sync", fv(C_CRA, 448), g["crpack"], "cnp")
        cm["tril"] = dma("sync", fv(C_TRIL, 64, 0, 64), g["ctril"], "cn")
        cm["trih"] = dma("sync", fv(C_TRIH, 64), g["ctrih"], "cn")
        cm["trili"] = dma("sync", MSK[0:64, 0:64], g["ctrili"], "cn")
        cm["trihi"] = dma("sync", MSK[:, 64:128], g["ctrihi"], "cn")
        cm["trif"] = dma("sync", MSK[:, 128:256], g["ctrif"], "cn")
        cm["r16"] = dma("sync", fv(C_R16, 16), g["cr16"], "cn")
        cm["gr4"] = dma("sync", fv(C_GR4, 4), g["cgr4"], "cn")
        cm["idt"] = dma("sync", fv(C_IDT, 128), g["cidt"], "cn")
        cm["gw"] = dma("sync", fv(C_GW, 128).rearrange("p (dc e) -> p dc e", dc=8),
                       g["gwT"].rearrange("(dc p) e -> p dc e", p=128), "cn")
        cm["gb"] = dma("sync", fv(C_GBROW, 16, 0, 1), g["gbrow"], "cn")
        dma("sync", fv(C_TF, NEC), g["tfcT"], "cn")

    wmark = {}
    for nm, off, sem in (("wkT", B_WK, "wk"), ("wvT", B_WV, "wv"), ("wrT", B_WR, "wr"),
                         ("wgT", B_WG, "wg"), ("woT", B_WO, "wo")):
        wmark[nm] = dma("gpsimd", bv(off, 8192).rearrange("p (dc e) -> p dc e", dc=8),
                        g[nm].rearrange("(dc p) e -> p dc e", p=128), sem)

    p.emit("vector", "memset", fv(C_ONESC, 1), 1.0)
    p.emit("vector", "memset", fv(C_ONESR, 128, 0, 1), 1.0)
    p.emit("vector", "memset", fv(C_EPSC, 1), 1e-5)
    p.emit("vector", "memset", fv(C_NEG1, 1), -1.0)
    p.emit("vector", "memset", fv(C_Z64, 128), 0.0)
    p.emit("vector", "memset", fv(A_S, 512), 0.0)
    p.emit("vector", "memset", bv(B_KT, 2048), 0.0)
    p.emit("vector", "memset", bv(B_SB, 512), 0.0)
    m_memset = p.emit("vector", "memset", bv(B_ATM, 256), 0.0, _inc="dv")
    if STOP == "pow":
        dbg_dump([("pa", fv(A_PA, 1024)), ("pbh", fv(A_PBH, 1024)), ("pq", fv(A_PQ, 1024)),
                  ("pbl", fv(A_PBL, 512)), ("pdt", fv(A_PDT, 1024)), ("wcc", fv(C_WCC, 8)),
                  ("td", fv(C_TD, 8)), ("tm", fv(C_TM, 8)), ("tf", fv(C_TF, 8))])

    STG_OFF = [A_STG, 16384]
    STGv = [fv(A_STG, 2048), fv(16384, 2048)]
    STG3v = [t.rearrange("p (dc t) -> p dc t", dc=8) for t in STGv]
    XXBv = [None, None]
    XXBv[0] = bv(B_XXB, 2048)
    XXBv[1] = bv(B_XXB2, 2048)
    xxb_rd2 = [Res(), Res()]
    vtok = [Res(), Res()]
    khat = [Res(), Res()]
    stg_rd = []
    xxb_rd = Res()
    m_chunk_mms = {}
    sb_pending = []
    m_stt = [None] * NEC
    pair_last = [None] * NEC
    m_supd = [None] * NEC
    m_wkv_pair = {}
    atm_res = [Res(), Res()]
    state = {"aob": None, "stats": None, "spill": None, "proj": {}}

    def scan_chunk(c, prefix):
        par = c % 2
        lc = par
        m_scal = {}
        m_kt = {}
        if not prefix:
            for hp in range(NEC):
                KF = fv(A_KFEAT + hp * SCW + lc * 128, 128)
                if pair_last[hp] is not None:
                    p.wait("gpsimd", pair_last[hp])
                p.wait("gpsimd", state["proj"][hp])
                p.emit("gpsimd", "tensor_tensor", bv(B_APAIR + hp * 128, 128), KF,
                       fv(A_PA + hp * 128, 128), ALU.mult, _inc="gp")
                p.emit("gpsimd", "tensor_tensor", bv(B_BL + hp * 64, 64), KF[:, 0:64],
                       fv(A_PBL + hp * 64, 64), ALU.mult, _inc="gp")
                mbh = p.emit("gpsimd", "tensor_tensor", bv(B_BH + hp * 128, 128), KF,
                             fv(A_PBH + hp * 128, 128), ALU.mult, _inc="gp")
                m_scal[hp] = mbh
                last = None
                for h in (2 * hp, 2 * hp + 1):
                    rw = (h % 2) * 64
                    last = p.emit("gpsimd", "tensor_tensor",
                                  bv(B_KT + h * 128, 128, rw, 64), KF[rw:rw + 64, :],
                                  fv(A_PQ + hp * 128, 128, rw, 64), ALU.mult, _inc="gp")
                m_kt[hp] = last
                pool_rd[hp] = last
        last_mm = None
        for hp in range(NEC):
            heads = (2 * hp, 2 * hp + 1)
            if not prefix:
                for h in heads:
                    rw = (h % 2) * 64
                    ba = banks[6 + (h % 2)]
                    p.wait("tensor", ba.r)
                    p.wait("tensor", m_scal[hp])
                    p.emit("tensor", "matmul", ba.ap[0:64, 0:64],
                           bv(B_BL + hp * 64, 64, rw, 64),
                           bv(B_APAIR + hp * 128, 64, rw, 64), start=True, stop=True)
                    mat = p.emit("tensor", "matmul", ba.ap[:, 64:128],
                                 bv(B_BH + hp * 128, 128, rw, 64),
                                 bv(B_APAIR + hp * 128 + 64, 64, rw, 64),
                                 start=True, stop=True, _inc="mm")
                    ba.w = mat
                    atm = atm_res[h % 2]
                    p.wait("vector", mat)
                    if atm.r is not None:
                        p.wait("vector", atm.r)
                    ao = B_ATM + (h % 2) * 128
                    mev = p.emit("vector", "select", bv(ao, 128), MSK[:, 128:256],
                                 ba.ap[:, 0:128], fv(C_Z64, 128), _inc="dv")
                    ba.r = mev
                    atm.w = mev
                bw = banks[hp % 2]
                p.wait("tensor", bw.r)
                p.wait("tensor", khat[par].w)
                p.wait("tensor", vtok[par].w)
                if m_supd[hp] is not None:
                    p.wait("tensor", m_supd[hp])
                mwk = None
                p.wait("tensor", m_kt[hp])
                for h in heads:
                    rw = (h % 2) * 64
                    atm = atm_res[h % 2]
                    p.wait("tensor", atm.w)
                    p.emit("tensor", "matmul", bw.ap[rw:rw + 64, 0:128],
                           bv(B_VTOK + par * 1024 + h * 64, 64), bv(B_ATM + (h % 2) * 128, 128),
                           start=True, stop=False)
                    mwk = p.emit("tensor", "matmul", bw.ap[rw:rw + 64, 0:128],
                                 bv(B_SB + hp * 64, 64), bv(B_KT + h * 128, 128),
                                 start=False, stop=True, _inc="mm")
                    atm.r = mwk
                m_wkv_pair[hp] = mwk
                pair_last[hp] = mwk
                bw.w = mwk
                KF = fv(A_KFEAT + hp * SCW + lc * 128, 128)
                VF = bv(B_VFEAT + hp * SCW + lc * 128, 128)
                p.wait("vector", mwk)
                p.emit("vector", "scalar_tensor_tensor", TF1, KF, fv(C_TF + hp, 1), VF,
                       ALU.mult, ALU.mult, _inc="dv")
                p.emit("vector", "tensor_tensor", TF2, bw.ap[:, 0:128], TF1, ALU.add, _inc="dv")
                mao = p.emit("vector", "tensor_tensor", bv(B_AOB + hp * SCW + lc * 128, 128),
                             TF2, fv(A_RG + hp * SCW + lc * 128, 128), ALU.mult, _inc="dv")
                bw.r = mao
                state["aob"] = mao
                scan_rd[hp] = mao
            bs = banks[6 + hp % 2]
            p.wait("tensor", bs.r)
            p.wait("tensor", khat[par].w)
            p.wait("tensor", vtok[par].w)
            mst = None
            for h in heads:
                rw = (h % 2) * 64
                mst = p.emit("tensor", "matmul", bs.ap[rw:rw + 64, 0:64],
                             bv(B_KHAT + par * 1024 + h * 64, 64),
                             bv(B_VTOK + par * 1024 + h * 64, 64),
                             start=True, stop=True, _inc="mm")
            bs.w = mst
            last_mm = mst
            if not prefix:
                p.wait("vector", m_wkv_pair[hp])
            p.wait("vector", mst)
            mstt = p.emit("vector", "scalar_tensor_tensor", fv(A_S + hp * 64, 64),
                          fv(A_S + hp * 64, 64), fv(C_WCC + hp, 1), bs.ap[:, 0:64],
                          ALU.mult, ALU.add, _inc="dv")
            bs.r = mstt
            m_stt[hp] = mstt
            sb_pending.append(hp)
        for hp in sb_pending:
            p.wait("gpsimd", m_stt[hp])
            msu = p.emit("gpsimd", "tensor_copy", bv(B_SB + hp * 64, 64),
                         fv(A_S + hp * 64, 64), _inc="gp")
            m_supd[hp] = msu
        sb_pending.clear()
        m_chunk_mms[c] = last_mm
        vtok[par].r = last_mm
        khat[par].r = last_mm

    spill_mark = [None, None]
    stg_rd2 = [[], []]
    scan_rd = [None] * NEC
    pool_rd = [None] * NEC

    def st_load(s, prefix):
        slot = s % 2
        STG = STGv[slot]
        STG3 = STG3v[slot]
        xsrc = g["xTpre"] if prefix else g["xT"]
        for m in stg_rd2[slot]:
            p.wait("sync", m)
        stg_rd2[slot] = []
        p.wait("sync", spill_mark[slot])
        mx = dma("sync", STG3,
                 xsrc.rearrange("(dc p) t -> p dc t", p=128)[:, :, s * SCW:(s + 1) * SCW],
                 "sg%d" % slot)
        XXB = XXBv[slot]
        p.wait("scalar", mx)
        p.wait("scalar", m_tm)
        p.wait("scalar", xxb_rd2[slot].r)
        mxx = None
        for dc in range(NEC):
            mxx = p.emit("scalar", "activation", XXB[:, dc * SCW:(dc + 1) * SCW],
                         STG[:, dc * SCW:(dc + 1) * SCW], AF.Copy,
                         scale=fv(C_TM + dc, 1), _inc="ac")
        if prefix:
            stg_rd2[slot].append(mxx)
        state["xxb_w"] = mxx
        state["xxb"] = XXB

    def st_tok(s, base, prefix):
        xxb_w = state["xxb_w"]
        XXB = state["xxb"]
        last_tok = None
        for tt in range(2):
            c = base + 2 * s + tt
            par = c % 2
            prev = m_chunk_mms.get(c - 2)
            for hf in range(2):
                for pj in ("k", "v"):
                    bk = banks[6 + (tt * 2 + hf) % 2]
                    woff = B_WK if pj == "k" else B_WV
                    mms = [(XXB[:, dc * SCW + tt * 128: dc * SCW + tt * 128 + 128],
                            bv(woff + dc * 1024 + hf * 512, 512)) for dc in range(NEC)]
                    mk = mm_group(bk, bk.ap[:, 0:512], mms,
                                  pre=(xxb_w, wmark["wkT" if pj == "k" else "wvT"]))
                    last_tok = mk
                    if not prefix and s == NSC - 1:
                        state["wk_dead" if pj == "k" else "wv_dead"] = mk
                    if pj == "k":
                        p.wait("vector", mk)
                        if prev is not None:
                            p.wait("vector", prev)
                        mkh = p.emit("vector", "tensor_tensor",
                                     bv(B_KHAT + par * 1024 + hf * 512, 512),
                                     bk.ap[:, 0:512], fv(A_PDT + hf * 512, 512), ALU.mult,
                                     _inc="dv")
                        bk.r = mkh
                        khat[par].w = mkh
                    else:
                        p.wait("scalar", mk)
                        if prev is not None:
                            p.wait("scalar", prev)
                        mv = p.emit("scalar", "activation", bv(B_VTOK + par * 1024 + hf * 512, 512),
                                    bk.ap[:, 0:512], AF.Copy, _inc="ac")
                        bk.r = mv
                        vtok[par].w = mv
        xxb_rd2[s % 2].r = last_tok

    def st_feat(s):
        xxb_w = state["xxb_w"]
        XXB = state["xxb"]
        for ec in range(NEC):
            bkg = banks[2 + (ec * 4) % 4]
            bkr = banks[2 + (ec * 4 + 1) % 4]
            bkk = banks[2 + (ec * 4 + 2) % 4]
            bkv = banks[2 + (ec * 4 + 3) % 4]

            def fmm(woff):
                return [(bv(woff + dc * 1024 + ec * 128, 128), XXB[:, dc * SCW:(dc + 1) * SCW])
                        for dc in range(NEC)]

            mg = mm_group(bkg, bkg.ap[:, 0:SCW], fmm(B_WG), pre=(xxb_w, wmark["wgT"]))
            mr = mm_group(bkr, bkr.ap[:, 0:SCW], fmm(B_WR), pre=(wmark["wrT"],))
            mk_ = mm_group(bkk, bkk.ap[:, 0:SCW], fmm(B_WK), pre=())
            mv_ = mm_group(bkv, bkv.ap[:, 0:SCW], fmm(B_WV), pre=())
            p.wait("scalar", scan_rd[ec])
            p.wait("scalar", pool_rd[ec])
            p.wait("scalar", mg)
            msil = p.emit("scalar", "activation", fv(A_RG + ec * SCW, SCW), bkg.ap[:, 0:SCW],
                          AF.Silu, _inc="ac")
            bkg.r = msil
            p.wait("vector", mr)
            p.wait("vector", msil)
            mrg = p.emit("vector", "tensor_tensor", fv(A_RG + ec * SCW, SCW), bkr.ap[:, 0:SCW],
                         fv(A_RG + ec * SCW, SCW), ALU.mult, _inc="dv")
            bkr.r = mrg
            p.wait("scalar", mk_)
            mkf = p.emit("scalar", "activation", fv(A_KFEAT + ec * SCW, SCW), bkk.ap[:, 0:SCW],
                         AF.Copy, _inc="ac")
            bkk.r = mkf
            p.wait("scalar", mv_)
            mvf = p.emit("scalar", "activation", bv(B_VFEAT + ec * SCW, SCW), bkv.ap[:, 0:SCW],
                         AF.Copy, _inc="ac")
            bkv.r = mvf
            state["proj"][ec] = mkf
            state["proj_extra", ec] = (mrg, mvf)

    def st_scan(s, base, prefix):
        if not prefix:
            for hp in range(NEC):
                mrg, mvf = state["proj_extra", hp]
                p.wait("vector", mvf)
        for tt in range(2):
            scan_chunk(base + 2 * s + tt, prefix)

    def st_tail(s):
        slot = s % 2
        STG = STGv[slot]
        STG3 = STG3v[slot]
        m_z1 = {}
        for ec in range(NEC):
            bk = banks[4 + ec % 2]
            mms = [(bv(B_WO + dc * 1024 + ec * 128, 128), bv(B_AOB + dc * SCW, SCW))
                   for dc in range(NEC)]
            mk = mm_group(bk, bk.ap[:, 0:SCW], mms, pre=(state["aob"], wmark["woT"]))
            p.wait("vector", mk)
            mz = p.emit("vector", "tensor_tensor", STG[:, ec * SCW:(ec + 1) * SCW],
                        bk.ap[:, 0:SCW], STG[:, ec * SCW:(ec + 1) * SCW], ALU.add, _inc="dv")
            bk.r = mz
            m_z1[ec] = mz
        bk1, bk2 = banks[2], banks[3]
        p.wait("tensor", bk1.r)
        p.wait("tensor", bk2.r)
        SQ1 = fv(A_TF, SCW)
        SQ2 = fv(A_TF + SCW, SCW)
        last1 = last2 = None
        sqmm = [None, None]
        for dc in range(NEC):
            p.wait("vector", m_z1[dc])
            sq = SQ1 if dc % 2 == 0 else SQ2
            p.wait("vector", sqmm[dc % 2])
            msq = p.emit("vector", "tensor_tensor", sq, STG[:, dc * SCW:(dc + 1) * SCW],
                         STG[:, dc * SCW:(dc + 1) * SCW], ALU.mult, _inc="dv")
            p.wait("tensor", m_z1[dc])
            last1 = p.emit("tensor", "matmul", bk1.ap[0:1, 0:SCW], fv(C_ONESC, 1),
                           STG[:, dc * SCW:(dc + 1) * SCW], start=(dc == 0),
                           stop=(dc == NEC - 1), _inc=("mm" if dc == NEC - 1 else None))
            p.wait("tensor", msq)
            last2 = p.emit("tensor", "matmul", bk2.ap[0:1, 0:SCW], fv(C_ONESC, 1), sq,
                           start=(dc == 0), stop=(dc == NEC - 1), _inc="mm")
            sqmm[dc % 2] = ("mm", p.cnt["mm"])
        bk1.w, bk2.w = last1, last2
        p.wait("vector", last1)
        mrow = p.emit("vector", "tensor_scalar", fv(A_MROW + s * SCW, SCW, 0, 1),
                      bk1.ap[0:1, 0:SCW], 1.0 / 1024.0, None, ALU.mult, _inc="dv")
        bk1.r = mrow
        p.wait("vector", mrow)
        msqr = p.emit("vector", "tensor_tensor", fv(A_MSQ, SCW, 0, 1),
                      fv(A_MROW + s * SCW, SCW, 0, 1), fv(A_MROW + s * SCW, SCW, 0, 1),
                      ALU.mult, _inc="dv")
        p.wait("vector", last2)
        p.wait("vector", msqr)
        mvar = p.emit("vector", "scalar_tensor_tensor", fv(A_VROW + s * SCW, SCW, 0, 1),
                      bk2.ap[0:1, 0:SCW], 1.0 / 1024.0, fv(A_MSQ, SCW, 0, 1),
                      ALU.mult, ALU.subtract, _inc="dv")
        bk2.r = mvar
        state["stats"] = (last2, mvar)
        mkl0 = None
        mkv_last = None
        for tt in range(2):
            gtt = s * 2 + tt
            bk = banks[4 + tt]
            mms = [(STG[:, dc * SCW + tt * 128: dc * SCW + tt * 128 + 128],
                    fv(C_GW + dc * 16, 16)) for dc in range(NEC)]
            mkl0 = mm_group(bk, bk.ap[:, 0:16], mms, pre=(m_z1[NEC - 1], m_consts))
            p.wait("vector", mkl0)
            ml0 = p.emit("vector", "tensor_copy", fv(C_SCORES + gtt * 16, 16),
                         bk.ap[:, 0:16], _inc="dv")
            bk.r = ml0
            bkc = banks[2 + tt]
            mkc = mm_single(bkc, bkc.ap[:, 0:1],
                            FB[0:1, A_MROW + s * SCW + tt * 128: A_MROW + s * SCW + tt * 128 + 128],
                            fv(C_NEG1, 1, 0, 1), pre=(mrow,))
            p.wait("vector", mkc)
            mmc = p.emit("vector", "tensor_copy", fv(C_MC + gtt, 1), bkc.ap[:, 0:1], _inc="dv")
            bkc.r = mmc
            mkv = mm_single(bkc, bkc.ap[:, 1:2],
                            FB[0:1, A_VROW + s * SCW + tt * 128: A_VROW + s * SCW + tt * 128 + 128],
                            fv(C_ONESC, 1, 0, 1), pre=(mvar,))
            mkv_last = mkv
            p.wait("vector", mkv)
            mvc = p.emit("vector", "tensor_copy", fv(C_VC + gtt, 1), bkc.ap[:, 1:2], _inc="dv")
            bkc.r = mvc
            state["mvc"] = mvc
        VRS = fv(A_VROW + s * SCW, SCW, 0, 1)
        bkmb, bkrs = banks[2], banks[3]
        mkmb = mm_single(bkmb, bkmb.ap[:, 256:512], fv(C_ONESR, 128, 0, 1),
                         fv(A_MROW + s * SCW, SCW, 0, 1), pre=(mrow,))
        mkrs = mm_single(bkrs, bkrs.ap[:, 256:512], fv(C_ONESR, 128, 0, 1), VRS,
                         pre=(mvar,))
        RSB1 = fv(B_RSB + (s % 2) * SCW, SCW)
        p.wait("scalar", mkrs)
        p.wait("scalar", state.get(("mx1l", s - 2)))
        msd1 = p.emit("scalar", "activation", RSB1, bkrs.ap[:, 256:512], AF.Sqrt,
                      bias=fv(C_EPSC, 1), _inc="ac")
        bkrs.r = msd1
        p.wait("vector", msd1)
        mrs1 = p.emit("vector", "reciprocal", RSB1, RSB1, _inc="dv")
        p.wait("vector", mkmb)
        p.wait("vector", last2)
        p.wait("vector", mkl0)
        mx1l = None
        for ec in range(NEC):
            p.emit("vector", "tensor_tensor", STG[:, ec * SCW:(ec + 1) * SCW],
                   STG[:, ec * SCW:(ec + 1) * SCW], bkmb.ap[:, 256:512], ALU.subtract,
                   _inc="dv")
            mx1l = p.emit("vector", "tensor_tensor", STG[:, ec * SCW:(ec + 1) * SCW],
                          STG[:, ec * SCW:(ec + 1) * SCW], RSB1, ALU.mult,
                          _inc="dv")
        bkmb.r = mx1l
        stg_rd2[slot].extend([last2, mvar, mkl0, mx1l])
        state["mx1l", s] = mx1l
        p.wait("sync", mx1l)
        msp = dma("sync",
                  x1sp.rearrange("(dc p) t -> p dc t", p=128)[:, :, s * SCW:(s + 1) * SCW],
                  STG3, "sp%d" % slot)
        if s < NSC - 2:
            p.wait("gpsimd", mx1l)
            mspb = dma("gpsimd",
                       x1bsp.rearrange("(dc p) t -> p dc t", p=128)[:, :, s * SCW:(s + 1) * SCW],
                       STG3, "rl%d" % s)
            state["spill_b", s] = mspb
            stg_rd2[slot].append(("rl%d" % s, 16))
        spill_mark[slot] = msp
        state["spill"] = msp

    _EARLY = ("prefix", "sc0", "attn", "xx", "tok", "feat", "scan")
    st_load(0, True)
    init_rest()
    m_cnp = p.marker("cnp")
    m_consts = p.marker("cn")

    TF1 = fv(A_TF, 128)
    TF2 = fv(A_TF + 128, 128)
    p.wait("scalar", m_cnp)
    for ec in range(NEC):
        tdcol = fv(C_TD + ec, 1)
        for src, dst, w in ((C_CRA, A_PA + ec * 128, 128), (C_CRBH, A_PBH + ec * 128, 128),
                            (C_CRQ, A_PQ + ec * 128, 128), (C_CRBL, A_PBL + ec * 64, 64)):
            p.emit("scalar", "activation", fv(dst, w), fv(src, w), AF.Exp, scale=tdcol, _inc="ac")
    m_wcc = p.emit("scalar", "activation", fv(C_WCC, 8), fv(C_TD, 8), AF.Exp, scale=-128.0, _inc="ac")
    m_pdt = None
    for hf in range(2):
        bk = banks[4 + hf]
        mk = mm_single(bk, pb[4 + hf][:, 0:512], fv(C_ONESR, 128, 0, 1),
                       fv(C_TDROW + hf * 512, 512, 0, 1), pre=(m_cnp, m_memset))
        p.wait("scalar", mk)
        m_pdt = p.emit("scalar", "activation", fv(A_PDT + hf * 512, 512), pb[4 + hf][:, 0:512],
                       AF.Exp, scale=fv(C_CJC, 1), _inc="ac")
        bk.r = m_pdt
    p.wait("vector", m_pdt)
    for s in range(NSC):
        if s > 0:
            st_load(s, True)
        st_tok(s, 0, True)
        st_scan(s, 0, True)
    if STOP == "prefix":
        dbg_dump([("s", fv(A_S, 512)), ("sb", bv(B_SB, 512)),
                  ("khat", bv(B_KHAT, 2048)), ("vtok", bv(B_VTOK, 2048)),
                  ("xxb", bv(B_XXB, 2048))])
    else:
        nmain = 1 if STOP in ("sc0", "xx", "tok", "feat", "scan") else NSC
        p.wait("vector", m_consts)
        p.wait("scalar", m_consts)
        p.wait("tensor", m_consts)
        bkg2 = banks[4]
        p.wait("tensor", bkg2.r)
        p.wait("tensor", m_consts)
        lastg = None
        for dc in range(NEC):
            lastg = p.emit("tensor", "matmul", bkg2.ap[0:1, 0:16], fv(C_ONESC, 1),
                           fv(C_GW + dc * 16, 16), start=(dc == 0), stop=(dc == NEC - 1),
                           _inc=("mm" if dc == NEC - 1 else None))
        bkg2.w = lastg
        p.wait("vector", lastg)
        mgrow = p.emit("vector", "tensor_copy", fv(C_GROW, 16, 0, 1), bkg2.ap[0:1, 0:16],
                       _inc="dv")
        bkg2.r = mgrow
        p.wait("tensor", mgrow)
        mkb = mm_single(bkg2, bkg2.ap[:, 0:16], fv(C_ONESR, 128, 0, 1), fv(C_GROW, 16, 0, 1))
        p.wait("vector", mkb)
        mgb2 = p.emit("vector", "tensor_copy", fv(C_GB2, 16), bkg2.ap[:, 0:16], _inc="dv")
        bkg2.r = mgb2
        st_load(0, False)
        st_feat(0)
        st_tok(0, NSC * 2, False)
        for s in range(nmain):
            st_scan(s, NSC * 2, False)
            if s + 1 < NSC and STOP not in ("xx", "tok", "feat", "scan"):
                st_load(s + 1, False)
                st_feat(s + 1)
            st_tail(s)
            if s + 1 < NSC and STOP not in ("xx", "tok", "feat", "scan"):
                st_tok(s + 1, NSC * 2, False)
            if s == NSC - 2 and STOP not in ("xx", "tok", "feat", "scan"):
                x1br3 = x1bsp.rearrange("(dc p) t -> p dc t", p=128)
                nready = NSC - 2
                for dc in range(NEC):
                    p.wait("sync", state["wk_dead" if dc < 4 else "wv_dead"])
                    for sp_ in range(nready):
                        p.wait("sync", state["spill_b", sp_])
                    dma("sync", bv(B_X1B + dc * 2048, nready * SCW),
                        x1br3[:, dc, 0:nready * SCW], "xr%d" % dc)
            if STOP in ("xx", "tok", "feat", "scan"):
                break
        attn_done_mm = p.marker("mm")
        if STOP == "sc0":
            dbg_dump([("kfeat", fv(A_KFEAT, 2048)), ("rg", fv(A_RG, 2048)),
                      ("vfeat", bv(B_VFEAT, 2048)), ("xxb", bv(B_XXB, 2048)),
                      ("aob", bv(B_AOB, 2048)), ("s", fv(A_S, 512)),
                      ("z1", fv(A_STG, 2048)), ("apair", bv(B_APAIR, 1024)),
                      ("atm", bv(B_ATM, 256)), ("khat", bv(B_KHAT, 2048)),
                      ("vtok", bv(B_VTOK, 2048)), ("mrow", fv(A_MROW, 2048, 0, 1)),
                      ("vrow", fv(A_VROW, 2048, 0, 1))])
        elif STOP == "attn":
            dbg_dump([("mrow", fv(A_MROW, 2048, 0, 1)), ("vrow", fv(A_VROW, 2048, 0, 1)),
                      ("mc", fv(C_MC, 16)), ("vc", fv(C_VC, 16)),
                      ("scores", fv(C_SCORES, 256))])

    def _materialize_stop():
        def runner(engname):
            def run(eng):
                for item in p.q[engname]:
                    if item[0] == "w":
                        eng.wait_ge(sems[item[1]], item[2])
                    else:
                        _, fn, args, kwargs, inc, incn = item
                        r = getattr(eng, fn)(*args, **kwargs)
                        if inc:
                            r.then_inc(sems[inc], incn)
            return run
        with nc.Block() as block:
            block.sync(runner("sync"))
            block.gpsimd(runner("gpsimd"))
            block.scalar(runner("scalar"))
            block.vector(runner("vector"))
            block.tensor(runner("tensor"))
        ctx.close()

    if STOP in _EARLY or STOP == "pow":
        _materialize_stop()
        return nc
    p.wait("sync", m_chunk_mms[NSC * 4 - 1])
    p.wait("sync", state["stats"][0])
    m_x1b_cp_last = None
    for sp_ in (NSC - 2, NSC - 1):
        STGc = STGv[sp_ % 2]
        p.wait("scalar", state["mx1l", sp_])
        for ec in range(NEC):
            p.wait("scalar", state["wk_dead" if ec < 4 else "wv_dead"])
            m_x1b_cp_last = p.emit("scalar", "activation",
                                   bv(B_X1B + ec * 2048 + sp_ * SCW, SCW),
                                   STGc[:, ec * SCW:(ec + 1) * SCW], AF.Copy, _inc="ac")
    x1b_ready = [("xr%d" % dc, 16) for dc in range(NEC)] + [m_x1b_cp_last]
    bk = banks[0]
    mk = mm_single(bk, bk.ap[:, 0:16], fv(C_ONESR, 128, 0, 1), fv(C_GBROW, 16, 0, 1),
                   pre=(cm["gb"],))
    p.wait("vector", mk)
    mgbb = p.emit("vector", "tensor_copy", fv(C_GBB, 16), bk.ap[:, 0:16], _inc="dv")
    bk.r = mgbb
    m_cw = None
    p.wait("scalar", state["stats"][1])
    p.wait("scalar", state.get("mvc"))
    msdg = p.emit("scalar", "activation", fv(C_SD2, 16), fv(C_VC, 16), AF.Sqrt,
                  bias=fv(C_EPSC, 1), _inc="ac")
    p.wait("vector", msdg)
    mrsg = p.emit("vector", "reciprocal", fv(C_RSC, 16), fv(C_SD2, 16), _inc="dv")
    p.wait("vector", mrsg)
    msc_last = None
    for ttl in range(NTT):
        if msc_last is not None:
            p.wait("vector", msc_last)
        p.emit("vector", "scalar_tensor_tensor", fv(C_SD2, 16), fv(C_GB2, 16),
               fv(C_MC + ttl, 1), fv(C_SCORES + ttl * 16, 16), ALU.mult, ALU.add)
        p.fence("vector", "dv")
        mpre = p.emit("vector", "tensor_scalar", fv(C_SD2, 16), fv(C_SD2, 16),
                      fv(C_RSC + ttl, 1), None, ALU.mult, _inc="dv")
        p.wait("scalar", mpre)
        msc_last = p.emit("scalar", "activation", fv(C_SCORES + ttl * 16, 16), fv(C_SD2, 16),
                          AF.Sigmoid, _inc="ac")
    m_gate_mm = p.marker("mm")

    m_cw_box = [None]
    gate_steps = []

    def gstep(f):
        gate_steps.append(f)

    def plan_gate_logic():
        def fence():
            p.fence("vector", "dv")

        def s_sfc():
            p.wait("vector", msc_last)
            p.wait("vector", mgbb)
            p.wait("vector", spill_mark[1])
            p.wait("vector", m_x1b_cp_last)
            for ttl in range(NTT):
                p.emit("vector", "tensor_tensor", fv(GT_SFC + ttl * 16, 16),
                       fv(C_SCORES + ttl * 16, 16), fv(C_GBB, 16), ALU.add)
            fence()
        gstep(s_sfc)
        first = [True]
        for i in range(4):
            for j in range(i + 1, 4):
                def s_pair(i=i, j=j):
                    tgt = GT_GS if first[0] else GT_GK
                    for ttl in range(NTT):
                        p.emit("vector", "tensor_tensor", fv(tgt + ttl * 4, 4),
                               FB[:, GT_SFC + ttl * 16 + i: GT_SFC + ttl * 16 + 16: 4],
                               FB[:, GT_SFC + ttl * 16 + j: GT_SFC + ttl * 16 + 16: 4], ALU.add)
                    fence()
                    if not first[0]:
                        for ttl in range(NTT):
                            p.emit("vector", "tensor_tensor", fv(GT_GS + ttl * 4, 4),
                                   fv(GT_GS + ttl * 4, 4), fv(GT_GK + ttl * 4, 4), ALU.max)
                        fence()
                    first[0] = False
                gstep(s_pair)
        def s_gkey():
            for ttl in range(NTT):
                p.emit("vector", "tensor_tensor", fv(GT_GK + ttl * 4, 4), fv(GT_GS + ttl * 4, 4),
                       fv(C_GR4, 4), ALU.subtract)
            fence()
        gstep(s_gkey)
        for gg in range(4):
            def s_cnt(gg=gg):
                for ttl in range(NTT):
                    if gg == 0:
                        p.emit("vector", "tensor_scalar", fv(GT_CNT + ttl * 4, 4),
                               fv(GT_GK + ttl * 4, 4), fv(GT_GK + ttl * 4, 1), None, ALU.is_lt)
                    else:
                        p.emit("vector", "scalar_tensor_tensor", fv(GT_CNT + ttl * 4, 4),
                               fv(GT_GK + ttl * 4, 4), fv(GT_GK + ttl * 4 + gg, 1),
                               fv(GT_CNT + ttl * 4, 4), ALU.is_lt, ALU.add)
                fence()
            gstep(s_cnt)
        def s_sel4():
            p.emit("vector", "tensor_scalar", fv(GT_SEL, 64), fv(GT_CNT, 64), 1.5, None, ALU.is_le)
            fence()
        gstep(s_sel4)
        for gg in range(4):
            def s_key(gg=gg):
                for ttl in range(NTT):
                    p.emit("vector", "tensor_scalar", fv(GT_KEY + ttl * 16 + 4 * gg, 4),
                           fv(GT_SFC + ttl * 16 + 4 * gg, 4), fv(GT_SEL + ttl * 4 + gg, 1),
                           None, ALU.mult)
                if gg == 3:
                    fence()
            gstep(s_key)
        def s_ramp():
            for ttl in range(NTT):
                p.emit("vector", "tensor_tensor", fv(GT_KEY + ttl * 16, 16),
                       fv(GT_KEY + ttl * 16, 16), fv(C_R16, 16), ALU.subtract)
            fence()
        gstep(s_ramp)
        for e in range(16):
            def s_rank(e=e):
                for ttl in range(NTT):
                    if e == 0:
                        p.emit("vector", "tensor_scalar", fv(GT_CWU + ttl * 16, 16),
                               fv(GT_KEY + ttl * 16, 16), fv(GT_KEY + ttl * 16, 1), None,
                               ALU.is_lt)
                    else:
                        p.emit("vector", "scalar_tensor_tensor", fv(GT_CWU + ttl * 16, 16),
                               fv(GT_KEY + ttl * 16, 16), fv(GT_KEY + ttl * 16 + e, 1),
                               fv(GT_CWU + ttl * 16, 16), ALU.is_lt, ALU.add)
                fence()
            gstep(s_rank)
        def s_tail():
            p.emit("vector", "tensor_scalar", fv(GT_CWU, 256), fv(GT_CWU, 256), 3.5, None,
                   ALU.is_le)
            p.fence("vector", "dv")
            p.emit("vector", "tensor_tensor", fv(GT_CWU, 256), fv(GT_CWU, 256),
                   fv(C_SCORES, 256), ALU.mult)
            p.fence("vector", "dv")
            for ttl in range(NTT):
                p.emit("vector", "reduce_sum", fv(GT_WS + ttl, 1), fv(GT_CWU + ttl * 16, 16),
                       axis=AX.X)
            p.fence("vector", "dv")
            p.emit("vector", "tensor_scalar", fv(GT_WS, 16), fv(GT_WS, 16), 1e-20, None, ALU.add)
            p.fence("vector", "dv")
            p.emit("vector", "reciprocal", fv(GT_WI, 16), fv(GT_WS, 16))
            p.fence("vector", "dv")
            m_cw = None
            for ttl in range(NTT):
                m_cw = p.emit("vector", "tensor_scalar", fv(C_CW + ttl * 16, 16),
                              fv(GT_CWU + ttl * 16, 16), fv(GT_WI + ttl, 1), None, ALU.mult,
                              _inc="dv")
            m_cw_box[0] = m_cw
        gstep(s_tail)
    p.wait("vector", m_gate_mm)
    if STOP == "gate":
        dbg_dump([("scores", fv(C_SCORES, 256)), ("mc", fv(C_MC, 16)),
                  ("vc", fv(C_VC, 16)), ("rsc", fv(C_RSC, 16)), ("gb2", fv(C_GB2, 16)),
                  ("mrow", fv(A_MROW, 2048, 0, 1)), ("vrow", fv(A_VROW, 2048, 0, 1)),
                  ("x1b", bv(B_X1B, 16384))])
        _materialize_stop()
        return nc

    p.wait("gpsimd", attn_done_mm)
    wgu_slot = [Res() for _ in range(6)]
    wd_slot = [Res() for _ in range(NIC)]
    silu_res = [Res(), Res()]
    st_e = {"m_down_last": None, "myd": None, "m_cw": None}

    def wsrc_gu(e, ic, which):
        if e < NEXP:
            t = g["gpT"] if which == 0 else g["upT"]
            return t[e][:, ic * 128:(ic + 1) * 128].rearrange("(dc p) i -> p dc i", p=128)
        hf = e - NEXP
        t = g["sgT"] if which == 0 else g["suT"]
        return t[:, hf * I_EXP + ic * 128: hf * I_EXP + (ic + 1) * 128].rearrange(
            "(dc p) i -> p dc i", p=128)

    def wsrc_d(e, ic):
        if e < NEXP:
            return g["dpT"][e][ic * 128:(ic + 1) * 128, :]
        hf = e - NEXP
        return g["sdT"][hf * I_EXP + ic * 128: hf * I_EXP + (ic + 1) * 128, :]

    dma_map = {}
    all_pairs = [(e, ic) for e in range(NPSEUDO) for ic in range(NIC)]

    def emit_gu_dma(pair_idx):
        e, ic = all_pairs[pair_idx]
        ms = []
        for which in range(2):
            sl_i = (pair_idx * 2 + which) % 6
            sl = wgu_slot[sl_i]
            p.wait("gpsimd", sl.r)
            ms.append((sl_i,
                       dma("gpsimd",
                           bv(B_WGU + sl_i * 1024, 1024).rearrange("p (dc i) -> p dc i", dc=8),
                           wsrc_gu(e, ic, which), "g%d" % sl_i)))
        dma_map[(e, ic)] = ms

    for i in range(3):
        emit_gu_dma(i)

    def plan_wd(e):
        wd_m = []
        for ic in range(NIC):
            p.wait("gpsimd", wd_slot[ic].r)
            wd_m.append(dma("gpsimd", bv(B_WD + ic * 1024, 1024), wsrc_d(e, ic), "d%d" % ic))
        return wd_m

    def plan_gu(e, steps=None):
        m_act_last = None
        for ic in range(NIC):
            if steps:
                for _ in range(4):
                    if steps:
                        steps.pop(0)()
            (slg, mg), (slu, mu) = dma_map.pop((e, ic))
            for half in range(2):
                t0, t1 = 2 * half, 2 * half + 1
                bset = 0 if (ic * 2 + half) % 2 == 0 else 4
                bg0, bg1 = banks[bset], banks[bset + 1]
                bu0, bu1 = banks[bset + 2], banks[bset + 3]
                for bk in (bg0, bg1, bu0, bu1):
                    p.wait("tensor", bk.r)
                p.wait("tensor", mg)
                p.wait("tensor", mu)
                for _xm in x1b_ready:
                    p.wait("tensor", _xm)
                for dc in range(NEC):
                    st, sp = (dc == 0), (dc == NEC - 1)
                    wgt = bv(B_WGU + slg * 1024 + dc * 128, 128)
                    wut = bv(B_WGU + slu * 1024 + dc * 128, 128)
                    x0 = bv(B_X1B + dc * 2048 + t0 * 512, 512)
                    x1v = bv(B_X1B + dc * 2048 + t1 * 512, 512)
                    p.emit("tensor", "matmul", bg0.ap[:, 0:512], wgt, x0, start=st, stop=sp,
                           _inc=("mm" if sp else None))
                    p.emit("tensor", "matmul", bg1.ap[:, 0:512], wgt, x1v, start=st, stop=sp,
                           _inc=("mm" if sp else None))
                    p.emit("tensor", "matmul", bu0.ap[:, 0:512], wut, x0, start=st, stop=sp,
                           _inc=("mm" if sp else None))
                    mker = p.emit("tensor", "matmul", bu1.ap[:, 0:512], wut, x1v, start=st,
                                  stop=sp, _inc=("mm" if sp else None))
                bg0.w = bg1.w = bu0.w = bu1.w = mker
                for bg, bu, tau in ((bg0, bu0, t0), (bg1, bu1, t1)):
                    sres = silu_res[tau % 2]
                    p.wait("scalar", bg.w)
                    if sres.r is not None:
                        p.wait("scalar", sres.r)
                    msil = p.emit("scalar", "activation", bv(B_SILU + (tau % 2) * 512, 512),
                                  bg.ap[:, 0:512], AF.Silu, _inc="ac")
                    bg.r = msil
                    sres.w = msil
                    p.wait("vector", bu.w)
                    p.wait("vector", msil)
                    if ic == 0 and half == 0 and tau == t0:
                        p.wait("vector", st_e["m_down_last"])
                    mact = p.emit("vector", "tensor_tensor",
                                  bv(B_ACT + ic * 2048 + tau * 512, 512),
                                  bu.ap[:, 0:512], bv(B_SILU + (tau % 2) * 512, 512), ALU.mult,
                                  _inc="dv")
                    bu.r = mact
                    sres.r = mact
                if half == 1:
                    wgu_slot[slg].r = mker
                    wgu_slot[slu].r = mker
                    if ic == NIC - 1:
                        m_act_last = mact
            nxt = e * NIC + ic + 3
            if nxt < len(all_pairs):
                emit_gu_dma(nxt)
        return m_act_last

    def plan_down(e, m_act_last, wd_m, tile_cb=None, first=False):
        myd = None
        for ttl in range(NTT):
            bset = 4 if ttl % 2 == 0 else 6
            b0, b1 = banks[bset], banks[bset + 1]
            p.wait("tensor", b0.r)
            p.wait("tensor", b1.r)
            p.wait("tensor", m_act_last)
            for _m in wd_m:
                p.wait("tensor", _m)
            p.wait("tensor", st_e["m_cw"])
            for ic in range(NIC):
                st, sp = (ic == 0), (ic == NIC - 1)
                a = bv(B_ACT + ic * 2048 + ttl * 128, 128)
                p.emit("tensor", "matmul", b0.ap[:, 0:512], a,
                       bv(B_WD + ic * 1024, 512), start=st, stop=sp,
                       _inc=("mm" if sp else None))
                mker = p.emit("tensor", "matmul", b1.ap[:, 0:512], a,
                              bv(B_WD + ic * 1024 + 512, 512), start=st, stop=sp,
                              _inc=("mm" if sp else None))
            b0.w = b1.w = mker
            for hh, bk in ((0, b0), (1, b1)):
                p.wait("vector", bk.w)
                if first and ttl == 0 and hh == 0:
                    p.wait("vector", spill_mark[0])
                    p.wait("vector", spill_mark[1])
                ysl = FB[:, ttl * 1024 + hh * 512: ttl * 1024 + hh * 512 + 512]
                if first:
                    myd = p.emit("vector", "tensor_scalar", ysl, bk.ap[:, 0:512],
                                 fv(C_CW + ttl * 16 + e, 1), None, ALU.mult, _inc="dv")
                elif e < NEXP:
                    myd = p.emit("vector", "scalar_tensor_tensor", ysl, bk.ap[:, 0:512],
                                 fv(C_CW + ttl * 16 + e, 1), ysl, ALU.mult, ALU.add, _inc="dv")
                else:
                    myd = p.emit("vector", "tensor_tensor", ysl, bk.ap[:, 0:512], ysl,
                                 ALU.add, _inc="dv")
                bk.r = myd
            if tile_cb is not None:
                tile_cb(ttl, myd)
        st_e["myd"] = myd
        st_e["m_down_last"] = p.marker("mm")
        for ic in range(NIC):
            wd_slot[ic].r = st_e["m_down_last"]

    plan_gate_logic()
    wd_m0 = plan_wd(0)
    m_act0 = plan_gu(0, gate_steps)
    while gate_steps:
        gate_steps.pop(0)()
    st_e["m_cw"] = m_cw_box[0]
    plan_down(0, m_act0, wd_m0, first=True)
    for e in range(1, NPSEUDO - 1):
        wd_m = plan_wd(e)
        m_act = plan_gu(e)
        plan_down(e, m_act, wd_m)

    xe_slot = [Res(), Res()]
    xe_slot[0].r = m_cw_box[0]
    xe_slot[1].r = m_cw_box[0]
    p.wait("sync", spill_mark[0])
    p.wait("sync", spill_mark[1])
    p.wait("sync", m_x1b_cp_last)
    outs_slot = [Res(), Res()]
    x1r = x1sp.rearrange("(dc p) t -> p dc t", p=128)

    L2C1S = C_SCORES
    L2C2S = C_SCORES + 16
    L2MS = C_SCORES + 32
    L2SDS = C_SCORES + 48
    end_state = {"sq": None, "mza": {}}

    def end_tile(ttl, m_ytile):
        sl = xe_slot[ttl % 2]
        p.wait("sync", sl.r)
        mxe = dma("sync", fv(E_XE + (ttl % 2) * 1024, 1024).rearrange("p (dc t) -> p dc t", dc=8),
                  x1r[:, :, ttl * 128:(ttl + 1) * 128], "xe%d" % (ttl % 2))
        mza = None
        mtr = None
        for dc in range(NEC):
            bk = banks[dc % 2]
            p.wait("tensor", bk.r)
            p.wait("tensor", mxe)
            p.wait("tensor", m_consts)
            mtr = p.emit("tensor", "transpose", bk.ap[:, 0:128],
                         fv(E_XE + (ttl % 2) * 1024 + dc * 128, 128), fv(C_IDT, 128), _inc="mm")
            bk.w = mtr
            ysl = FB[:, ttl * 1024 + dc * 128: ttl * 1024 + dc * 128 + 128]
            p.wait("vector", mtr)
            p.wait("vector", m_ytile)
            mza = p.emit("vector", "tensor_tensor", ysl, bk.ap[:, 0:128], ysl, ALU.add, _inc="dv")
            bk.r = mza
        sl.r = mtr
        YT = FB[:, ttl * 1024:(ttl + 1) * 1024]
        p.wait("scalar", mza)
        msq2 = p.emit("scalar", "activation", fv(E_SQ, 1024), YT, AF.Square,
                      accum_out=fv(L2C2S + ttl, 1), _inc="ac")
        p.emit("vector", "reduce_sum", fv(L2C1S + ttl, 1), YT, axis=AX.X, _inc="dv")
        end_state["sq"] = msq2
        end_state["mza"][ttl] = mza
        if ttl == 7:
            end_finish_range(0, 8)
        elif ttl == 15:
            end_finish_range(8, 16)

    def end_finish_range(lo, hi):
        n = hi - lo
        p.fence("vector", "dv")
        p.wait("vector", end_state["sq"])
        p.emit("vector", "tensor_scalar", fv(L2MS + lo, n), fv(L2C1S + lo, n), 1.0 / 1024.0,
               None, ALU.mult)
        p.fence("vector", "dv")
        p.emit("vector", "tensor_tensor", fv(L2SDS + lo, n), fv(L2MS + lo, n), fv(L2MS + lo, n),
               ALU.mult)
        p.fence("vector", "dv")
        mvar = p.emit("vector", "scalar_tensor_tensor", fv(L2SDS + lo, n), fv(L2C2S + lo, n),
                      1.0 / 1024.0, fv(L2SDS + lo, n), ALU.mult, ALU.subtract, _inc="dv")
        p.wait("scalar", mvar)
        msd = p.emit("scalar", "activation", fv(L2SDS + 16 + lo, n), fv(L2SDS + lo, n), AF.Sqrt,
                     bias=fv(C_EPSC, 1), _inc="ac")
        p.wait("vector", msd)
        p.emit("vector", "reciprocal", fv(L2SDS + 32 + lo, n), fv(L2SDS + 16 + lo, n))
        p.fence("vector", "dv")
        for ttl in range(lo, hi):
            YT = FB[:, ttl * 1024:(ttl + 1) * 1024]
            osl = outs_slot[ttl % 2]
            if osl.r is not None:
                p.wait("vector", osl.r)
            mo = p.emit("vector", "tensor_scalar", fv(E_OUTS + (ttl % 2) * 1024, 1024), YT,
                        fv(L2MS + ttl, 1), fv(L2SDS + 32 + ttl, 1), ALU.subtract, ALU.mult,
                        _inc="dv")
            p.wait("sync", mo)
            mst = dma("sync", out[ttl * 128:(ttl + 1) * 128, :],
                      fv(E_OUTS + (ttl % 2) * 1024, 1024), "ou%d" % (ttl % 2))
            osl.r = mst

    wd_m17 = plan_wd(NPSEUDO - 1)
    m_act17 = plan_gu(NPSEUDO - 1)
    plan_down(NPSEUDO - 1, m_act17, wd_m17, tile_cb=end_tile)
    if STOP == "fin":
        dbg_dump([("y", FB[:, 0:16384]), ("l2", fv(GS, 16)),
                  ("outs", fv(E_OUTS, 2048)), ("esq", fv(E_SQ, 1024)),
                  ("cw", fv(C_CW, 256))])
        _materialize_stop()
        return nc
    for snm in ["ou0", "ou1", "sp0", "sp1"] + ["rl%d" % i for i in range(NSC - 2)]:
        p.wait("sync", p.marker(snm))

    def runner(engname):
        def run(eng):
            for item in p.q[engname]:
                if item[0] == "w":
                    eng.wait_ge(sems[item[1]], item[2])
                else:
                    _, fn, args, kwargs, inc, incn = item
                    r = getattr(eng, fn)(*args, **kwargs)
                    if inc:
                        r.then_inc(sems[inc], incn)
        return run

    with nc.Block() as block:
        block.sync(runner("sync"))
        block.gpsimd(runner("gpsimd"))
        block.scalar(runner("scalar"))
        block.vector(runner("vector"))
        block.tensor(runner("tensor"))
    ctx.close()
    return nc


_NC_CACHE = None


def make_inmaps(inputs):
    f32 = np.float32
    x = np.asarray(inputs["L0_attn_x"], f32)
    tm = np.asarray(inputs["L0_attn_time_mix"], f32).reshape(-1)
    td = np.asarray(inputs["L0_attn_time_decay"], f32).reshape(-1)
    tf = np.asarray(inputs["L0_attn_time_first"], f32).reshape(-1)

    share = {}
    for nm, key in (("wkT", "L0_attn_Wk"), ("wvT", "L0_attn_Wv"), ("wrT", "L0_attn_Wr"),
                    ("wgT", "L0_attn_Wg"), ("woT", "L0_attn_Wo")):
        share[nm] = np.ascontiguousarray(np.asarray(inputs[key], f32).T)
    share["gwT"] = np.ascontiguousarray(np.asarray(inputs["L0_ffn_gate_w"], f32).T)
    share["gbrow"] = np.asarray(inputs["L0_ffn_gate_bias"], f32).reshape(1, 16)
    share["gpT"] = np.ascontiguousarray(np.asarray(inputs["L0_ffn_gate_proj"], f32).transpose(0, 2, 1))
    share["upT"] = np.ascontiguousarray(np.asarray(inputs["L0_ffn_up_proj"], f32).transpose(0, 2, 1))
    share["dpT"] = np.ascontiguousarray(np.asarray(inputs["L0_ffn_down_proj"], f32).transpose(0, 2, 1))
    share["sgT"] = np.ascontiguousarray(np.asarray(inputs["L0_ffn_shared_gate_w"], f32).T)
    share["suT"] = np.ascontiguousarray(np.asarray(inputs["L0_ffn_shared_up_w"], f32).T)
    share["sdT"] = np.ascontiguousarray(np.asarray(inputs["L0_ffn_shared_down_w"], f32).T)
    share["tdcT"] = np.ascontiguousarray(td.reshape(NEC, 128).T)
    share["tfcT"] = np.ascontiguousarray(tf.reshape(NEC, 128).T)
    share["tmcT"] = np.ascontiguousarray(tm.reshape(NEC, 128).T)
    share["tdrow"] = td.reshape(1, D).copy()
    ii = np.arange(128, dtype=f32)
    cra = np.where(ii < 64, -(ii - 31.5), -(ii - 95.5)).astype(f32)
    share["crpack"] = np.concatenate([
        np.broadcast_to(cra, (128, 128)),
        np.broadcast_to(-(95.5 - ii), (128, 128)).astype(f32),
        np.broadcast_to(-(ii + 1.0), (128, 128)).astype(f32),
        np.broadcast_to(-(31.5 - ii[:64]), (128, 64)).astype(f32)], axis=1).copy()
    share["cjc"] = (-(127.0 - ii)).astype(f32).reshape(128, 1)
    jj = np.arange(64, dtype=f32)
    share["ctril"] = (jj[:, None] <= jj[None, :]).astype(f32)
    share["ctrih"] = (ii[:, None] <= (jj[None, :] + 64.0)).astype(f32)
    share["ctrili"] = (jj[:, None] <= jj[None, :]).astype(np.uint8)
    share["ctrihi"] = (ii[:, None] <= (jj[None, :] + 64.0)).astype(np.uint8)
    share["ctrif"] = (ii[:, None] <= ii[None, :]).astype(np.uint8)
    share["cr16"] = np.broadcast_to(np.arange(16, dtype=f32) * DELTA, (128, 16)).copy()
    share["cgr4"] = np.broadcast_to(np.arange(4, dtype=f32) * DELTA, (128, 4)).copy()
    share["cidt"] = np.eye(128, dtype=f32)

    zeros_pre = np.zeros((D, TLOC), f32)
    in_maps = []
    for core in range(8):
        b, half = core // 2, core % 2
        xTb = np.ascontiguousarray(x[b].T)
        m = dict(share)
        m["xT"] = np.ascontiguousarray(xTb[:, half * TLOC:(half + 1) * TLOC])
        m["xTpre"] = np.ascontiguousarray(xTb[:, 0:TLOC]) if half == 1 else zeros_pre
        in_maps.append(m)
    return in_maps


def kernel(**inputs):
    global _NC_CACHE
    f32 = np.float32
    in_maps = make_inmaps(inputs)
    if _NC_CACHE is None:
        _NC_CACHE = build_nc()
    res = run_bass_kernel_spmd(_NC_CACHE, in_maps, core_ids=list(range(8)))
    outp = np.empty((4, 4096, D), f32)
    if DEBUG:
        kernel.debug_results = res.results
    for core in range(8):
        b, half = core // 2, core % 2
        outp[b, half * TLOC:(half + 1) * TLOC, :] = np.asarray(res.results[core]["out"])
    return outp


# version 22
# speedup vs baseline: 1.0213x; 1.0156x over previous
import os
import numpy as np
from contextlib import ExitStack

import concourse.bass as bass
import concourse.mybir as mybir
from concourse.bass_utils import run_bass_kernel_spmd

F32 = mybir.dt.float32
BF16 = mybir.dt.bfloat16
AF = mybir.ActivationFunctionType
ALU = mybir.AluOpType
AX = mybir.AxisListType

DEBUG = bool(int(os.environ.get("MK_DEBUG", "0")))
STOP = os.environ.get("MK_STOP", "")

D = 1024
TLOC = 2048
NSC = 8
SCW = 256
NH = 16
HD = 64
NEC = 8
NEXP = 16
I_EXP = 1408
NIC = 11
NTT = 16
NPSEUDO = 18
DELTA = float(2.0 ** -40)

A_KFEAT = 0
A_RG = 2048
A_PA = 4096
A_PBH = 5120
A_PQ = 6144
A_PBL = 7168
A_PDT = 7680
A_STG = 8704
A_S = 10752
A_TF = 11264
A_MROW = 11776
A_VROW = 13824
A_MSQ = 15872
C_CRA = 11776
C_CRBH = 11904
C_CRQ = 12032
C_CRBL = 12160
C_TDROW = 12224
B_MB = 16384
B_RSB = 18432
E_XE = 16384
E_SQ = 18432
E_OUTS = 19456
C_SCORES = 21504
C_CW = 21760
C_GBB = 22016
C_GW = 22032
C_TD = 22160
C_TF = 22168
C_TM = 22176
C_WCC = 22184
C_ONESC = 22192
C_ONESR = 22193
C_EPSC = 22321
C_TRIL = 22322
C_TRIH = 22386
C_R16 = 22450
C_GR4 = 22466
C_IDT = 22470
C_CJC = 22598
C_GBROW = 22599
C_Z64 = 22615
GS = 22743
L2_C1 = GS + 0
L2_C2 = GS + 1
L2_M = GS + 2
L2_T = GS + 3
L2_V = GS + 4
L2_SD = GS + 5
C_CJC2 = GS + 6
C_MC = GS + 16
C_VC = GS + 32
C_RSC = GS + 48
C_SD2 = GS + 64
C_GB2 = GS + 80
C_GROW = GS + 96
C_NEG1 = GS + 112
F32_N = GS + 128
GT_SFC = 16384
GT_KEY = 16640
GT_GS = 16896
GT_GK = 16960
GT_CNT = 17024
GT_SEL = 17088
GT_CWU = 17152
GT_WS = 17408
GT_WI = 17424

B_WK = 0
B_WV = 8192
B_WR = 16384
B_WG = 24576
B_WO = 32768
B_XXB = 40960
B_VTOK = 43008
B_KHAT = 45056
B_VFEAT = 47104
B_BL = 49152
B_BH = 50176
B_KT = 52224
B_APAIR = 54272
B_ATM = 55296
B_AOB = 55552
ATTN_B16_END = B_AOB + 2048
B_X1B = 0
B_ACT = 16384
B_WGU = 38912
B_WD = 45056
B_SILU = 56320
B_SB = 57600
B_XXB2 = 58112
B16_N = 60160
assert ATTN_B16_END <= B16_N
assert F32_N * 4 + B16_N * 2 <= 207 * 1024, (F32_N * 4 + B16_N * 2)


class Planner:
    ENGINES = ("sync", "gpsimd", "scalar", "vector", "tensor")

    def __init__(self):
        self.q = {e: [] for e in self.ENGINES}
        self.cnt = {}
        self.waited = {e: {} for e in self.ENGINES}

    def wait(self, eng, marker):
        if marker is None:
            return
        s, v = marker
        if self.waited[eng].get(s, 0) >= v:
            return
        self.waited[eng][s] = v
        q = self.q[eng]
        if q and q[-1][0] == "w" and q[-1][1] == s:
            q[-1] = ("w", s, max(q[-1][2], v))
        else:
            q.append(("w", s, v))

    def emit(self, eng, fn, *args, _inc=None, _incn=1, **kwargs):
        self.q[eng].append(("i", fn, args, kwargs, _inc, _incn))
        if _inc:
            self.cnt[_inc] = self.cnt.get(_inc, 0) + _incn
            return (_inc, self.cnt[_inc])
        return None

    def marker(self, sem):
        return (sem, self.cnt.get(sem, 0))

    def fence(self, eng, sem):
        for idx in range(len(self.q[eng]) - 1, -1, -1):
            item = self.q[eng][idx]
            if item[0] == "i":
                if item[4] is None:
                    self.cnt[sem] = self.cnt.get(sem, 0) + 1
                    self.q[eng][idx] = (item[0], item[1], item[2], item[3], sem, 1)
                    mk = (sem, self.cnt[sem])
                else:
                    mk = (item[4], self.cnt[item[4]])
                self.wait(eng, mk)
                return mk
        return None


class Bank:
    __slots__ = ("ap", "w", "r")

    def __init__(self, ap):
        self.ap = ap
        self.w = None
        self.r = None


class Res:
    __slots__ = ("w", "r")

    def __init__(self):
        self.w = None
        self.r = None


def build_nc():
    nc = bass.Bass("TRN2", target_bir_lowering=False)
    g = {}

    def din(name, shape, dt=F32):
        g[name] = nc.dram_tensor(name, shape, dt, kind="ExternalInput").ap()

    din("xT", [D, TLOC]); din("xTpre", [D, TLOC])
    for w in ("wkT", "wvT", "wrT", "wgT", "woT"):
        din(w, [D, D])
    din("tdcT", [128, NEC]); din("tfcT", [128, NEC]); din("tmcT", [128, NEC]); din("tdrow", [1, D])
    din("gwT", [D, 16]); din("gbrow", [1, 16])
    din("gpT", [NEXP, D, I_EXP]); din("upT", [NEXP, D, I_EXP]); din("dpT", [NEXP, I_EXP, D])
    din("sgT", [D, 2816]); din("suT", [D, 2816]); din("sdT", [2816, D])
    din("crpack", [128, 448])
    din("cjc", [128, 1]); din("ctril", [64, 64]); din("ctrih", [128, 64])
    g["ctrili"] = nc.dram_tensor("ctrili", [64, 64], mybir.dt.uint8, kind="ExternalInput").ap()
    g["ctrihi"] = nc.dram_tensor("ctrihi", [128, 64], mybir.dt.uint8, kind="ExternalInput").ap()
    g["ctrif"] = nc.dram_tensor("ctrif", [128, 128], mybir.dt.uint8, kind="ExternalInput").ap()
    din("cr16", [128, 16]); din("cgr4", [128, 4]); din("cidt", [128, 128])
    out = nc.dram_tensor("out", [TLOC, D], F32, kind="ExternalOutput").ap()
    zkind = "ExternalOutput" if DEBUG else "Internal"
    x1bsp = nc.dram_tensor("x1bsp", [D, TLOC], BF16, kind=zkind).ap()
    x1sp = nc.dram_tensor("x1sp", [D, TLOC], F32, kind=zkind).ap()

    ctx = ExitStack()
    FB = ctx.enter_context(nc.sbuf_tensor("FB", [128, F32_N], F32))
    BB = ctx.enter_context(nc.sbuf_tensor("BB", [128, B16_N], BF16))
    pb = [ctx.enter_context(nc.psum_tensor("pb%d" % i, [128, 512], F32)) for i in range(8)]
    MSK = ctx.enter_context(nc.sbuf_tensor("MSK", [128, 256], mybir.dt.uint8))
    semnames = (["mm", "dv", "ac", "cn", "cn2", "cnp", "wk", "wv", "wr", "wg", "wo",
                 "sg0", "sg1", "sp0", "sp1", "xe0", "xe1",
                 "ou0", "ou1", "dbg", "gp"]
                + ["g%d" % i for i in range(6)] + ["d%d" % i for i in range(11)]
                + ["rl%d" % i for i in range(8)] + ["xr%d" % i for i in range(8)])
    sems = {s: ctx.enter_context(nc.semaphore(s)) for s in semnames}

    p = Planner()
    banks = [Bank(pb[i]) for i in range(8)]

    def fv(off, w, p0=0, pn=128):
        return FB[p0:p0 + pn, off:off + w]

    def bv(off, w, p0=0, pn=128):
        return BB[p0:p0 + pn, off:off + w]

    def dma(queue, out_ap, in_ap, sem):
        return p.emit(queue, "dma_start", _inc=sem, _incn=16, out=out_ap, in_=in_ap)

    def mm_group(bank, out_ap, mms, pre=()):
        p.wait("tensor", bank.r)
        for m in pre:
            p.wait("tensor", m)
        mk = None
        n = len(mms)
        for i, (l, r) in enumerate(mms):
            mk = p.emit("tensor", "matmul", out_ap, l, r,
                        start=(i == 0), stop=(i == n - 1),
                        _inc=("mm" if i == n - 1 else None))
        bank.w = mk
        return mk

    dbg_tensors = []

    def dbg_dump(pairs):
        for eng_sem in semnames:
            if eng_sem != "dbg":
                p.wait("sync", p.marker(eng_sem))
        for name, ap in pairs:
            shp = [ap.shape[0], ap.shape[1]]
            t = nc.dram_tensor("dbg_" + name, shp, ap.tensor.dtype, kind="ExternalOutput").ap()
            dbg_tensors.append(name)
            dma("sync", t, ap, "dbg")
        p.wait("sync", p.marker("dbg"))

    def mm_single(bank, out_ap, l, r, pre=()):
        p.wait("tensor", bank.r)
        for m in pre:
            p.wait("tensor", m)
        mk = p.emit("tensor", "matmul", out_ap, l, r, start=True, stop=True, _inc="mm")
        bank.w = mk
        return mk

    cm = {}
    cm["tm"] = dma("sync", fv(C_TM, NEC), g["tmcT"], "cn2")
    m_tm = p.marker("cn2")
    state0_loaded = []

    def init_rest():
        cm["tdrow"] = dma("sync", fv(C_TDROW, D, 0, 1), g["tdrow"], "cnp")
        dma("sync", fv(C_TD, NEC), g["tdcT"], "cnp")
        cm["cjc"] = dma("sync", fv(C_CJC, 1), g["cjc"], "cnp")
        cm["crpack"] = dma("sync", fv(C_CRA, 448), g["crpack"], "cnp")
        cm["tril"] = dma("sync", fv(C_TRIL, 64, 0, 64), g["ctril"], "cn")
        cm["trih"] = dma("sync", fv(C_TRIH, 64), g["ctrih"], "cn")
        cm["trili"] = dma("sync", MSK[0:64, 0:64], g["ctrili"], "cn")
        cm["trihi"] = dma("sync", MSK[:, 64:128], g["ctrihi"], "cn")
        cm["trif"] = dma("sync", MSK[:, 128:256], g["ctrif"], "cn")
        cm["r16"] = dma("sync", fv(C_R16, 16), g["cr16"], "cn")
        cm["gr4"] = dma("sync", fv(C_GR4, 4), g["cgr4"], "cn")
        cm["idt"] = dma("sync", fv(C_IDT, 128), g["cidt"], "cn")
        cm["gw"] = dma("sync", fv(C_GW, 128).rearrange("p (dc e) -> p dc e", dc=8),
                       g["gwT"].rearrange("(dc p) e -> p dc e", p=128), "cn")
        cm["gb"] = dma("sync", fv(C_GBROW, 16, 0, 1), g["gbrow"], "cn")
        dma("sync", fv(C_TF, NEC), g["tfcT"], "cn")

    wmark = {}
    for nm, off, sem in (("wkT", B_WK, "wk"), ("wvT", B_WV, "wv"), ("wrT", B_WR, "wr"),
                         ("wgT", B_WG, "wg"), ("woT", B_WO, "wo")):
        wmark[nm] = dma("gpsimd", bv(off, 8192).rearrange("p (dc e) -> p dc e", dc=8),
                        g[nm].rearrange("(dc p) e -> p dc e", p=128), sem)

    p.emit("vector", "memset", fv(C_ONESC, 1), 1.0)
    p.emit("vector", "memset", fv(C_ONESR, 128, 0, 1), 1.0)
    p.emit("vector", "memset", fv(C_EPSC, 1), 1e-5)
    p.emit("vector", "memset", fv(C_NEG1, 1), -1.0)
    p.emit("vector", "memset", fv(C_Z64, 128), 0.0)
    p.emit("vector", "memset", fv(A_S, 512), 0.0)
    p.emit("vector", "memset", bv(B_KT, 2048), 0.0)
    p.emit("vector", "memset", bv(B_SB, 512), 0.0)
    m_memset = p.emit("vector", "memset", bv(B_ATM, 256), 0.0, _inc="dv")
    if STOP == "pow":
        dbg_dump([("pa", fv(A_PA, 1024)), ("pbh", fv(A_PBH, 1024)), ("pq", fv(A_PQ, 1024)),
                  ("pbl", fv(A_PBL, 512)), ("pdt", fv(A_PDT, 1024)), ("wcc", fv(C_WCC, 8)),
                  ("td", fv(C_TD, 8)), ("tm", fv(C_TM, 8)), ("tf", fv(C_TF, 8))])

    STG_OFF = [A_STG, 16384]
    STGv = [fv(A_STG, 2048), fv(16384, 2048)]
    STG3v = [t.rearrange("p (dc t) -> p dc t", dc=8) for t in STGv]
    XXBv = [None, None]
    XXBv[0] = bv(B_XXB, 2048)
    XXBv[1] = bv(B_XXB2, 2048)
    xxb_rd2 = [Res(), Res()]
    vtok = [Res(), Res()]
    khat = [Res(), Res()]
    stg_rd = []
    xxb_rd = Res()
    m_chunk_mms = {}
    sb_pending = []
    pending_tf = []
    m_stt = [None] * NEC
    pair_last = [None] * NEC
    m_supd = [None] * NEC
    m_wkv_pair = {}
    atm_res = [Res(), Res()]
    state = {"aob": None, "stats": None, "spill": None, "proj": {}}

    def scan_chunk(c, prefix):
        par = c % 2
        lc = par
        m_scal = {}
        m_kt = {}
        def drain_sb():
            for hp_ in sb_pending:
                p.wait("gpsimd", m_stt[hp_])
                msu = p.emit("gpsimd", "tensor_copy", bv(B_SB + hp_ * 64, 64),
                             fv(A_S + hp_ * 64, 64), _inc="gp")
                m_supd[hp_] = msu
            sb_pending.clear()
        if not prefix:
            for hp in range(NEC):
                KF = fv(A_KFEAT + hp * SCW + lc * 128, 128)
                if pair_last[hp] is not None:
                    p.wait("gpsimd", pair_last[hp])
                p.wait("gpsimd", state["proj"][hp])
                p.emit("gpsimd", "tensor_tensor", bv(B_APAIR + hp * 128, 128), KF,
                       fv(A_PA + hp * 128, 128), ALU.mult, _inc="gp")
                p.emit("gpsimd", "tensor_tensor", bv(B_BL + hp * 64, 64), KF[:, 0:64],
                       fv(A_PBL + hp * 64, 64), ALU.mult, _inc="gp")
                mbh = p.emit("gpsimd", "tensor_tensor", bv(B_BH + hp * 128, 128), KF,
                             fv(A_PBH + hp * 128, 128), ALU.mult, _inc="gp")
                m_scal[hp] = mbh
                last = None
                for h in (2 * hp, 2 * hp + 1):
                    rw = (h % 2) * 64
                    last = p.emit("gpsimd", "tensor_tensor",
                                  bv(B_KT + h * 128, 128, rw, 64), KF[rw:rw + 64, :],
                                  fv(A_PQ + hp * 128, 128, rw, 64), ALU.mult, _inc="gp")
                m_kt[hp] = last
                pool_rd[hp] = last
        drain_sb()
        last_mm = None
        for hp in range(NEC):
            heads = (2 * hp, 2 * hp + 1)
            if not prefix:
                for h in heads:
                    rw = (h % 2) * 64
                    ba = banks[6 + (h % 2)]
                    p.wait("tensor", ba.r)
                    p.wait("tensor", m_scal[hp])
                    p.emit("tensor", "matmul", ba.ap[0:64, 0:64],
                           bv(B_BL + hp * 64, 64, rw, 64),
                           bv(B_APAIR + hp * 128, 64, rw, 64), start=True, stop=True)
                    mat = p.emit("tensor", "matmul", ba.ap[:, 64:128],
                                 bv(B_BH + hp * 128, 128, rw, 64),
                                 bv(B_APAIR + hp * 128 + 64, 64, rw, 64),
                                 start=True, stop=True, _inc="mm")
                    ba.w = mat
                    atm = atm_res[h % 2]
                    p.wait("vector", mat)
                    if atm.r is not None:
                        p.wait("vector", atm.r)
                    ao = B_ATM + (h % 2) * 128
                    mev = p.emit("vector", "select", bv(ao, 128), MSK[:, 128:256],
                                 ba.ap[:, 0:128], fv(C_Z64, 128), _inc="dv")
                    ba.r = mev
                    atm.w = mev
                for tfb in pending_tf:
                    tfb()
                pending_tf.clear()
                bw = banks[hp % 2]
                p.wait("tensor", bw.r)
                p.wait("tensor", khat[par].w)
                p.wait("tensor", vtok[par].w)
                if m_supd[hp] is not None:
                    p.wait("tensor", m_supd[hp])
                mwk = None
                p.wait("tensor", m_kt[hp])
                for h in heads:
                    rw = (h % 2) * 64
                    atm = atm_res[h % 2]
                    p.wait("tensor", atm.w)
                    p.emit("tensor", "matmul", bw.ap[rw:rw + 64, 0:128],
                           bv(B_VTOK + par * 1024 + h * 64, 64), bv(B_ATM + (h % 2) * 128, 128),
                           start=True, stop=False)
                    mwk = p.emit("tensor", "matmul", bw.ap[rw:rw + 64, 0:128],
                                 bv(B_SB + hp * 64, 64), bv(B_KT + h * 128, 128),
                                 start=False, stop=True, _inc="mm")
                    atm.r = mwk
                m_wkv_pair[hp] = mwk
                pair_last[hp] = mwk
                bw.w = mwk

                def tf_block(hp=hp, lc=lc, mwk=mwk, bw=bw):
                    KF = fv(A_KFEAT + hp * SCW + lc * 128, 128)
                    VF = bv(B_VFEAT + hp * SCW + lc * 128, 128)
                    p.wait("vector", mwk)
                    p.emit("vector", "scalar_tensor_tensor", TF1, KF, fv(C_TF + hp, 1), VF,
                           ALU.mult, ALU.mult, _inc="dv")
                    p.emit("vector", "tensor_tensor", TF2, bw.ap[:, 0:128], TF1, ALU.add,
                           _inc="dv")
                    mao = p.emit("vector", "tensor_tensor",
                                 bv(B_AOB + hp * SCW + lc * 128, 128),
                                 TF2, fv(A_RG + hp * SCW + lc * 128, 128), ALU.mult,
                                 _inc="dv")
                    bw.r = mao
                    state["aob"] = mao
                    scan_rd[hp] = mao
                pending_tf.append(tf_block)
            bs = banks[6 + hp % 2]
            p.wait("tensor", bs.r)
            p.wait("tensor", khat[par].w)
            p.wait("tensor", vtok[par].w)
            mst = None
            for h in heads:
                rw = (h % 2) * 64
                mst = p.emit("tensor", "matmul", bs.ap[rw:rw + 64, 0:64],
                             bv(B_KHAT + par * 1024 + h * 64, 64),
                             bv(B_VTOK + par * 1024 + h * 64, 64),
                             start=True, stop=True, _inc="mm")
            bs.w = mst
            last_mm = mst
            if not prefix:
                p.wait("vector", m_wkv_pair[hp])
            p.wait("vector", mst)
            mstt = p.emit("vector", "scalar_tensor_tensor", fv(A_S + hp * 64, 64),
                          fv(A_S + hp * 64, 64), fv(C_WCC + hp, 1), bs.ap[:, 0:64],
                          ALU.mult, ALU.add, _inc="dv")
            bs.r = mstt
            m_stt[hp] = mstt
            sb_pending.append(hp)
        if prefix:
            drain_sb()
        for tfb in pending_tf:
            tfb()
        pending_tf.clear()
        m_chunk_mms[c] = last_mm
        vtok[par].r = last_mm
        khat[par].r = last_mm

    spill_mark = [None, None]
    stg_rd2 = [[], []]
    scan_rd = [None] * NEC
    pool_rd = [None] * NEC

    def st_load(s, prefix):
        slot = s % 2
        STG = STGv[slot]
        STG3 = STG3v[slot]
        xsrc = g["xTpre"] if prefix else g["xT"]
        for m in stg_rd2[slot]:
            p.wait("sync", m)
        stg_rd2[slot] = []
        p.wait("sync", spill_mark[slot])
        mx = dma("sync", STG3,
                 xsrc.rearrange("(dc p) t -> p dc t", p=128)[:, :, s * SCW:(s + 1) * SCW],
                 "sg%d" % slot)
        XXB = XXBv[slot]
        p.wait("scalar", mx)
        p.wait("scalar", m_tm)
        p.wait("scalar", xxb_rd2[slot].r)
        mxx = None
        for dc in range(NEC):
            mxx = p.emit("scalar", "activation", XXB[:, dc * SCW:(dc + 1) * SCW],
                         STG[:, dc * SCW:(dc + 1) * SCW], AF.Copy,
                         scale=fv(C_TM + dc, 1), _inc="ac")
        if prefix:
            stg_rd2[slot].append(mxx)
        state["xxb_w"] = mxx
        state["xxb"] = XXB

    def st_tok(s, base, prefix):
        xxb_w = state["xxb_w"]
        XXB = state["xxb"]
        last_tok = None
        for tt in range(2):
            c = base + 2 * s + tt
            par = c % 2
            prev = m_chunk_mms.get(c - 2)
            for hf in range(2):
                for pj in ("k", "v"):
                    bk = banks[6 + (tt * 2 + hf) % 2]
                    woff = B_WK if pj == "k" else B_WV
                    mms = [(XXB[:, dc * SCW + tt * 128: dc * SCW + tt * 128 + 128],
                            bv(woff + dc * 1024 + hf * 512, 512)) for dc in range(NEC)]
                    mk = mm_group(bk, bk.ap[:, 0:512], mms,
                                  pre=(xxb_w, wmark["wkT" if pj == "k" else "wvT"]))
                    last_tok = mk
                    if not prefix and s == NSC - 1:
                        state["wk_dead" if pj == "k" else "wv_dead"] = mk
                    if pj == "k":
                        p.wait("vector", mk)
                        if prev is not None:
                            p.wait("vector", prev)
                        mkh = p.emit("vector", "tensor_tensor",
                                     bv(B_KHAT + par * 1024 + hf * 512, 512),
                                     bk.ap[:, 0:512], fv(A_PDT + hf * 512, 512), ALU.mult,
                                     _inc="dv")
                        bk.r = mkh
                        khat[par].w = mkh
                    else:
                        p.wait("scalar", mk)
                        if prev is not None:
                            p.wait("scalar", prev)
                        mv = p.emit("scalar", "activation", bv(B_VTOK + par * 1024 + hf * 512, 512),
                                    bk.ap[:, 0:512], AF.Copy, _inc="ac")
                        bk.r = mv
                        vtok[par].w = mv
        xxb_rd2[s % 2].r = last_tok

    def st_feat(s):
        xxb_w = state["xxb_w"]
        XXB = state["xxb"]
        for ec in range(NEC):
            bkg = banks[2 + (ec * 4) % 4]
            bkr = banks[2 + (ec * 4 + 1) % 4]
            bkk = banks[2 + (ec * 4 + 2) % 4]
            bkv = banks[2 + (ec * 4 + 3) % 4]

            def fmm(woff):
                return [(bv(woff + dc * 1024 + ec * 128, 128), XXB[:, dc * SCW:(dc + 1) * SCW])
                        for dc in range(NEC)]

            mg = mm_group(bkg, bkg.ap[:, 0:SCW], fmm(B_WG), pre=(xxb_w, wmark["wgT"]))
            mr = mm_group(bkr, bkr.ap[:, 0:SCW], fmm(B_WR), pre=(wmark["wrT"],))
            mk_ = mm_group(bkk, bkk.ap[:, 0:SCW], fmm(B_WK), pre=())
            mv_ = mm_group(bkv, bkv.ap[:, 0:SCW], fmm(B_WV), pre=())
            p.wait("scalar", scan_rd[ec])
            p.wait("scalar", pool_rd[ec])
            p.wait("scalar", mg)
            msil = p.emit("scalar", "activation", fv(A_RG + ec * SCW, SCW), bkg.ap[:, 0:SCW],
                          AF.Silu, _inc="ac")
            bkg.r = msil
            p.wait("vector", mr)
            p.wait("vector", msil)
            mrg = p.emit("vector", "tensor_tensor", fv(A_RG + ec * SCW, SCW), bkr.ap[:, 0:SCW],
                         fv(A_RG + ec * SCW, SCW), ALU.mult, _inc="dv")
            bkr.r = mrg
            p.wait("scalar", mk_)
            mkf = p.emit("scalar", "activation", fv(A_KFEAT + ec * SCW, SCW), bkk.ap[:, 0:SCW],
                         AF.Copy, _inc="ac")
            bkk.r = mkf
            p.wait("scalar", mv_)
            mvf = p.emit("scalar", "activation", bv(B_VFEAT + ec * SCW, SCW), bkv.ap[:, 0:SCW],
                         AF.Copy, _inc="ac")
            bkv.r = mvf
            state["proj"][ec] = mkf
            state["proj_extra", ec] = (mrg, mvf)

    def st_scan(s, base, prefix):
        if not prefix:
            for hp in range(NEC):
                mrg, mvf = state["proj_extra", hp]
                p.wait("vector", mvf)
        for tt in range(2):
            scan_chunk(base + 2 * s + tt, prefix)

    def st_tail(s):
        slot = s % 2
        STG = STGv[slot]
        STG3 = STG3v[slot]
        m_z1 = {}
        for ec in range(NEC):
            bk = banks[4 + ec % 2]
            mms = [(bv(B_WO + dc * 1024 + ec * 128, 128), bv(B_AOB + dc * SCW, SCW))
                   for dc in range(NEC)]
            mk = mm_group(bk, bk.ap[:, 0:SCW], mms, pre=(state["aob"], wmark["woT"]))
            p.wait("vector", mk)
            mz = p.emit("vector", "tensor_tensor", STG[:, ec * SCW:(ec + 1) * SCW],
                        bk.ap[:, 0:SCW], STG[:, ec * SCW:(ec + 1) * SCW], ALU.add, _inc="dv")
            bk.r = mz
            m_z1[ec] = mz
        bk1, bk2 = banks[2], banks[3]
        p.wait("tensor", bk1.r)
        p.wait("tensor", bk2.r)
        SQ1 = fv(A_TF, SCW)
        SQ2 = fv(A_TF + SCW, SCW)
        last1 = last2 = None
        sqmm = [None, None]
        for dc in range(NEC):
            p.wait("vector", m_z1[dc])
            sq = SQ1 if dc % 2 == 0 else SQ2
            p.wait("vector", sqmm[dc % 2])
            msq = p.emit("vector", "tensor_tensor", sq, STG[:, dc * SCW:(dc + 1) * SCW],
                         STG[:, dc * SCW:(dc + 1) * SCW], ALU.mult, _inc="dv")
            p.wait("tensor", m_z1[dc])
            last1 = p.emit("tensor", "matmul", bk1.ap[0:1, 0:SCW], fv(C_ONESC, 1),
                           STG[:, dc * SCW:(dc + 1) * SCW], start=(dc == 0),
                           stop=(dc == NEC - 1), _inc=("mm" if dc == NEC - 1 else None))
            p.wait("tensor", msq)
            last2 = p.emit("tensor", "matmul", bk2.ap[0:1, 0:SCW], fv(C_ONESC, 1), sq,
                           start=(dc == 0), stop=(dc == NEC - 1), _inc="mm")
            sqmm[dc % 2] = ("mm", p.cnt["mm"])
        bk1.w, bk2.w = last1, last2
        p.wait("vector", last1)
        mrow = p.emit("vector", "tensor_scalar", fv(A_MROW + s * SCW, SCW, 0, 1),
                      bk1.ap[0:1, 0:SCW], 1.0 / 1024.0, None, ALU.mult, _inc="dv")
        bk1.r = mrow
        p.wait("vector", mrow)
        msqr = p.emit("vector", "tensor_tensor", fv(A_MSQ, SCW, 0, 1),
                      fv(A_MROW + s * SCW, SCW, 0, 1), fv(A_MROW + s * SCW, SCW, 0, 1),
                      ALU.mult, _inc="dv")
        p.wait("vector", last2)
        p.wait("vector", msqr)
        mvar = p.emit("vector", "scalar_tensor_tensor", fv(A_VROW + s * SCW, SCW, 0, 1),
                      bk2.ap[0:1, 0:SCW], 1.0 / 1024.0, fv(A_MSQ, SCW, 0, 1),
                      ALU.mult, ALU.subtract, _inc="dv")
        bk2.r = mvar
        state["stats"] = (last2, mvar)
        mkl0 = None
        mkv_last = None
        for tt in range(2):
            gtt = s * 2 + tt
            bk = banks[4 + tt]
            mms = [(STG[:, dc * SCW + tt * 128: dc * SCW + tt * 128 + 128],
                    fv(C_GW + dc * 16, 16)) for dc in range(NEC)]
            mkl0 = mm_group(bk, bk.ap[:, 0:16], mms, pre=(m_z1[NEC - 1], m_consts))
            p.wait("vector", mkl0)
            ml0 = p.emit("vector", "tensor_copy", fv(C_SCORES + gtt * 16, 16),
                         bk.ap[:, 0:16], _inc="dv")
            bk.r = ml0
            bkc = banks[2 + tt]
            mkc = mm_single(bkc, bkc.ap[:, 0:1],
                            FB[0:1, A_MROW + s * SCW + tt * 128: A_MROW + s * SCW + tt * 128 + 128],
                            fv(C_NEG1, 1, 0, 1), pre=(mrow,))
            p.wait("vector", mkc)
            mmc = p.emit("vector", "tensor_copy", fv(C_MC + gtt, 1), bkc.ap[:, 0:1], _inc="dv")
            bkc.r = mmc
            mkv = mm_single(bkc, bkc.ap[:, 1:2],
                            FB[0:1, A_VROW + s * SCW + tt * 128: A_VROW + s * SCW + tt * 128 + 128],
                            fv(C_ONESC, 1, 0, 1), pre=(mvar,))
            mkv_last = mkv
            p.wait("vector", mkv)
            mvc = p.emit("vector", "tensor_copy", fv(C_VC + gtt, 1), bkc.ap[:, 1:2], _inc="dv")
            bkc.r = mvc
            state["mvc"] = mvc
        VRS = fv(A_VROW + s * SCW, SCW, 0, 1)
        bkmb, bkrs = banks[2], banks[3]
        mkmb = mm_single(bkmb, bkmb.ap[:, 256:512], fv(C_ONESR, 128, 0, 1),
                         fv(A_MROW + s * SCW, SCW, 0, 1), pre=(mrow,))
        mkrs = mm_single(bkrs, bkrs.ap[:, 256:512], fv(C_ONESR, 128, 0, 1), VRS,
                         pre=(mvar,))
        RSB1 = fv(B_RSB + (s % 2) * SCW, SCW)
        p.wait("scalar", mkrs)
        p.wait("scalar", state.get(("mx1l", s - 2)))
        msd1 = p.emit("scalar", "activation", RSB1, bkrs.ap[:, 256:512], AF.Sqrt,
                      bias=fv(C_EPSC, 1), _inc="ac")
        bkrs.r = msd1
        p.wait("vector", msd1)
        mrs1 = p.emit("vector", "reciprocal", RSB1, RSB1, _inc="dv")
        p.wait("vector", mkmb)
        p.wait("vector", last2)
        p.wait("vector", mkl0)
        mx1l = None
        for ec in range(NEC):
            p.emit("vector", "tensor_tensor", STG[:, ec * SCW:(ec + 1) * SCW],
                   STG[:, ec * SCW:(ec + 1) * SCW], bkmb.ap[:, 256:512], ALU.subtract,
                   _inc="dv")
            mx1l = p.emit("vector", "tensor_tensor", STG[:, ec * SCW:(ec + 1) * SCW],
                          STG[:, ec * SCW:(ec + 1) * SCW], RSB1, ALU.mult,
                          _inc="dv")
        bkmb.r = mx1l
        stg_rd2[slot].extend([last2, mvar, mkl0, mx1l])
        state["mx1l", s] = mx1l
        p.wait("sync", mx1l)
        msp = dma("sync",
                  x1sp.rearrange("(dc p) t -> p dc t", p=128)[:, :, s * SCW:(s + 1) * SCW],
                  STG3, "sp%d" % slot)
        if s < NSC - 2:
            p.wait("gpsimd", mx1l)
            mspb = dma("gpsimd",
                       x1bsp.rearrange("(dc p) t -> p dc t", p=128)[:, :, s * SCW:(s + 1) * SCW],
                       STG3, "rl%d" % s)
            state["spill_b", s] = mspb
            stg_rd2[slot].append(("rl%d" % s, 16))
        spill_mark[slot] = msp
        state["spill"] = msp

    _EARLY = ("prefix", "sc0", "attn", "xx", "tok", "feat", "scan")
    st_load(0, True)
    init_rest()
    m_cnp = p.marker("cnp")
    m_consts = p.marker("cn")

    TF1 = fv(A_TF, 128)
    TF2 = fv(A_TF + 128, 128)
    p.wait("scalar", m_cnp)
    for ec in range(NEC):
        tdcol = fv(C_TD + ec, 1)
        for src, dst, w in ((C_CRA, A_PA + ec * 128, 128), (C_CRBH, A_PBH + ec * 128, 128),
                            (C_CRQ, A_PQ + ec * 128, 128), (C_CRBL, A_PBL + ec * 64, 64)):
            p.emit("scalar", "activation", fv(dst, w), fv(src, w), AF.Exp, scale=tdcol, _inc="ac")
    m_wcc = p.emit("scalar", "activation", fv(C_WCC, 8), fv(C_TD, 8), AF.Exp, scale=-128.0, _inc="ac")
    m_pdt = None
    for hf in range(2):
        bk = banks[4 + hf]
        mk = mm_single(bk, pb[4 + hf][:, 0:512], fv(C_ONESR, 128, 0, 1),
                       fv(C_TDROW + hf * 512, 512, 0, 1), pre=(m_cnp, m_memset))
        p.wait("scalar", mk)
        m_pdt = p.emit("scalar", "activation", fv(A_PDT + hf * 512, 512), pb[4 + hf][:, 0:512],
                       AF.Exp, scale=fv(C_CJC, 1), _inc="ac")
        bk.r = m_pdt
    p.wait("vector", m_pdt)
    for s in range(NSC):
        if s > 0:
            st_load(s, True)
        st_tok(s, 0, True)
        st_scan(s, 0, True)
    if STOP == "prefix":
        dbg_dump([("s", fv(A_S, 512)), ("sb", bv(B_SB, 512)),
                  ("khat", bv(B_KHAT, 2048)), ("vtok", bv(B_VTOK, 2048)),
                  ("xxb", bv(B_XXB, 2048))])
    else:
        nmain = 1 if STOP in ("sc0", "xx", "tok", "feat", "scan") else NSC
        p.wait("vector", m_consts)
        p.wait("scalar", m_consts)
        p.wait("tensor", m_consts)
        bkg2 = banks[4]
        p.wait("tensor", bkg2.r)
        p.wait("tensor", m_consts)
        lastg = None
        for dc in range(NEC):
            lastg = p.emit("tensor", "matmul", bkg2.ap[0:1, 0:16], fv(C_ONESC, 1),
                           fv(C_GW + dc * 16, 16), start=(dc == 0), stop=(dc == NEC - 1),
                           _inc=("mm" if dc == NEC - 1 else None))
        bkg2.w = lastg
        p.wait("vector", lastg)
        mgrow = p.emit("vector", "tensor_copy", fv(C_GROW, 16, 0, 1), bkg2.ap[0:1, 0:16],
                       _inc="dv")
        bkg2.r = mgrow
        p.wait("tensor", mgrow)
        mkb = mm_single(bkg2, bkg2.ap[:, 0:16], fv(C_ONESR, 128, 0, 1), fv(C_GROW, 16, 0, 1))
        p.wait("vector", mkb)
        mgb2 = p.emit("vector", "tensor_copy", fv(C_GB2, 16), bkg2.ap[:, 0:16], _inc="dv")
        bkg2.r = mgb2
        st_load(0, False)
        st_feat(0)
        st_tok(0, NSC * 2, False)
        for s in range(nmain):
            st_scan(s, NSC * 2, False)
            if s + 1 < NSC and STOP not in ("xx", "tok", "feat", "scan"):
                st_load(s + 1, False)
                st_feat(s + 1)
            st_tail(s)
            if s + 1 < NSC and STOP not in ("xx", "tok", "feat", "scan"):
                st_tok(s + 1, NSC * 2, False)
            if s == NSC - 2 and STOP not in ("xx", "tok", "feat", "scan"):
                x1br3 = x1bsp.rearrange("(dc p) t -> p dc t", p=128)
                nready = NSC - 2
                for dc in range(NEC):
                    p.wait("sync", state["wk_dead" if dc < 4 else "wv_dead"])
                    for sp_ in range(nready):
                        p.wait("sync", state["spill_b", sp_])
                    dma("sync", bv(B_X1B + dc * 2048, nready * SCW),
                        x1br3[:, dc, 0:nready * SCW], "xr%d" % dc)
            if STOP in ("xx", "tok", "feat", "scan"):
                break
        attn_done_mm = p.marker("mm")
        if STOP == "sc0":
            dbg_dump([("kfeat", fv(A_KFEAT, 2048)), ("rg", fv(A_RG, 2048)),
                      ("vfeat", bv(B_VFEAT, 2048)), ("xxb", bv(B_XXB, 2048)),
                      ("aob", bv(B_AOB, 2048)), ("s", fv(A_S, 512)),
                      ("z1", fv(A_STG, 2048)), ("apair", bv(B_APAIR, 1024)),
                      ("atm", bv(B_ATM, 256)), ("khat", bv(B_KHAT, 2048)),
                      ("vtok", bv(B_VTOK, 2048)), ("mrow", fv(A_MROW, 2048, 0, 1)),
                      ("vrow", fv(A_VROW, 2048, 0, 1))])
        elif STOP == "attn":
            dbg_dump([("mrow", fv(A_MROW, 2048, 0, 1)), ("vrow", fv(A_VROW, 2048, 0, 1)),
                      ("mc", fv(C_MC, 16)), ("vc", fv(C_VC, 16)),
                      ("scores", fv(C_SCORES, 256))])

    def _materialize_stop():
        def runner(engname):
            def run(eng):
                for item in p.q[engname]:
                    if item[0] == "w":
                        eng.wait_ge(sems[item[1]], item[2])
                    else:
                        _, fn, args, kwargs, inc, incn = item
                        r = getattr(eng, fn)(*args, **kwargs)
                        if inc:
                            r.then_inc(sems[inc], incn)
            return run
        with nc.Block() as block:
            block.sync(runner("sync"))
            block.gpsimd(runner("gpsimd"))
            block.scalar(runner("scalar"))
            block.vector(runner("vector"))
            block.tensor(runner("tensor"))
        ctx.close()

    if STOP in _EARLY or STOP == "pow":
        _materialize_stop()
        return nc
    p.wait("sync", m_chunk_mms[NSC * 4 - 1])
    p.wait("sync", state["stats"][0])
    m_x1b_cp_last = None
    for sp_ in (NSC - 2, NSC - 1):
        STGc = STGv[sp_ % 2]
        p.wait("scalar", state["mx1l", sp_])
        for ec in range(NEC):
            p.wait("scalar", state["wk_dead" if ec < 4 else "wv_dead"])
            m_x1b_cp_last = p.emit("scalar", "activation",
                                   bv(B_X1B + ec * 2048 + sp_ * SCW, SCW),
                                   STGc[:, ec * SCW:(ec + 1) * SCW], AF.Copy, _inc="ac")
    x1b_ready = [("xr%d" % dc, 16) for dc in range(NEC)] + [m_x1b_cp_last]
    bk = banks[0]
    mk = mm_single(bk, bk.ap[:, 0:16], fv(C_ONESR, 128, 0, 1), fv(C_GBROW, 16, 0, 1),
                   pre=(cm["gb"],))
    p.wait("vector", mk)
    mgbb = p.emit("vector", "tensor_copy", fv(C_GBB, 16), bk.ap[:, 0:16], _inc="dv")
    bk.r = mgbb
    m_cw = None
    p.wait("scalar", state["stats"][1])
    p.wait("scalar", state.get("mvc"))
    msdg = p.emit("scalar", "activation", fv(C_SD2, 16), fv(C_VC, 16), AF.Sqrt,
                  bias=fv(C_EPSC, 1), _inc="ac")
    p.wait("vector", msdg)
    mrsg = p.emit("vector", "reciprocal", fv(C_RSC, 16), fv(C_SD2, 16), _inc="dv")
    p.wait("vector", mrsg)
    msc_last = None
    for ttl in range(NTT):
        if msc_last is not None:
            p.wait("vector", msc_last)
        p.emit("vector", "scalar_tensor_tensor", fv(C_SD2, 16), fv(C_GB2, 16),
               fv(C_MC + ttl, 1), fv(C_SCORES + ttl * 16, 16), ALU.mult, ALU.add)
        p.fence("vector", "dv")
        mpre = p.emit("vector", "tensor_scalar", fv(C_SD2, 16), fv(C_SD2, 16),
                      fv(C_RSC + ttl, 1), None, ALU.mult, _inc="dv")
        p.wait("scalar", mpre)
        msc_last = p.emit("scalar", "activation", fv(C_SCORES + ttl * 16, 16), fv(C_SD2, 16),
                          AF.Sigmoid, _inc="ac")
    m_gate_mm = p.marker("mm")

    m_cw_box = [None]
    gate_steps = []

    def gstep(f):
        gate_steps.append(f)

    def plan_gate_logic():
        def fence():
            p.fence("vector", "dv")

        def s_sfc():
            p.wait("vector", msc_last)
            p.wait("vector", mgbb)
            p.wait("vector", spill_mark[1])
            p.wait("vector", m_x1b_cp_last)
            for ttl in range(NTT):
                p.emit("vector", "tensor_tensor", fv(GT_SFC + ttl * 16, 16),
                       fv(C_SCORES + ttl * 16, 16), fv(C_GBB, 16), ALU.add)
            fence()
        gstep(s_sfc)
        first = [True]
        for i in range(4):
            for j in range(i + 1, 4):
                def s_pair(i=i, j=j):
                    tgt = GT_GS if first[0] else GT_GK
                    for ttl in range(NTT):
                        p.emit("vector", "tensor_tensor", fv(tgt + ttl * 4, 4),
                               FB[:, GT_SFC + ttl * 16 + i: GT_SFC + ttl * 16 + 16: 4],
                               FB[:, GT_SFC + ttl * 16 + j: GT_SFC + ttl * 16 + 16: 4], ALU.add)
                    fence()
                    if not first[0]:
                        for ttl in range(NTT):
                            p.emit("vector", "tensor_tensor", fv(GT_GS + ttl * 4, 4),
                                   fv(GT_GS + ttl * 4, 4), fv(GT_GK + ttl * 4, 4), ALU.max)
                        fence()
                    first[0] = False
                gstep(s_pair)
        def s_gkey():
            for ttl in range(NTT):
                p.emit("vector", "tensor_tensor", fv(GT_GK + ttl * 4, 4), fv(GT_GS + ttl * 4, 4),
                       fv(C_GR4, 4), ALU.subtract)
            fence()
        gstep(s_gkey)
        for gg in range(4):
            def s_cnt(gg=gg):
                for ttl in range(NTT):
                    if gg == 0:
                        p.emit("vector", "tensor_scalar", fv(GT_CNT + ttl * 4, 4),
                               fv(GT_GK + ttl * 4, 4), fv(GT_GK + ttl * 4, 1), None, ALU.is_lt)
                    else:
                        p.emit("vector", "scalar_tensor_tensor", fv(GT_CNT + ttl * 4, 4),
                               fv(GT_GK + ttl * 4, 4), fv(GT_GK + ttl * 4 + gg, 1),
                               fv(GT_CNT + ttl * 4, 4), ALU.is_lt, ALU.add)
                fence()
            gstep(s_cnt)
        def s_sel4():
            p.emit("vector", "tensor_scalar", fv(GT_SEL, 64), fv(GT_CNT, 64), 1.5, None, ALU.is_le)
            fence()
        gstep(s_sel4)
        for gg in range(4):
            def s_key(gg=gg):
                for ttl in range(NTT):
                    p.emit("vector", "tensor_scalar", fv(GT_KEY + ttl * 16 + 4 * gg, 4),
                           fv(GT_SFC + ttl * 16 + 4 * gg, 4), fv(GT_SEL + ttl * 4 + gg, 1),
                           None, ALU.mult)
                if gg == 3:
                    fence()
            gstep(s_key)
        def s_ramp():
            for ttl in range(NTT):
                p.emit("vector", "tensor_tensor", fv(GT_KEY + ttl * 16, 16),
                       fv(GT_KEY + ttl * 16, 16), fv(C_R16, 16), ALU.subtract)
            fence()
        gstep(s_ramp)
        for e in range(16):
            def s_rank(e=e):
                for ttl in range(NTT):
                    if e == 0:
                        p.emit("vector", "tensor_scalar", fv(GT_CWU + ttl * 16, 16),
                               fv(GT_KEY + ttl * 16, 16), fv(GT_KEY + ttl * 16, 1), None,
                               ALU.is_lt)
                    else:
                        p.emit("vector", "scalar_tensor_tensor", fv(GT_CWU + ttl * 16, 16),
                               fv(GT_KEY + ttl * 16, 16), fv(GT_KEY + ttl * 16 + e, 1),
                               fv(GT_CWU + ttl * 16, 16), ALU.is_lt, ALU.add)
                fence()
            gstep(s_rank)
        def s_tail():
            p.emit("vector", "tensor_scalar", fv(GT_CWU, 256), fv(GT_CWU, 256), 3.5, None,
                   ALU.is_le)
            p.fence("vector", "dv")
            p.emit("vector", "tensor_tensor", fv(GT_CWU, 256), fv(GT_CWU, 256),
                   fv(C_SCORES, 256), ALU.mult)
            p.fence("vector", "dv")
            for ttl in range(NTT):
                p.emit("vector", "reduce_sum", fv(GT_WS + ttl, 1), fv(GT_CWU + ttl * 16, 16),
                       axis=AX.X)
            p.fence("vector", "dv")
            p.emit("vector", "tensor_scalar", fv(GT_WS, 16), fv(GT_WS, 16), 1e-20, None, ALU.add)
            p.fence("vector", "dv")
            p.emit("vector", "reciprocal", fv(GT_WI, 16), fv(GT_WS, 16))
            p.fence("vector", "dv")
            m_cw = None
            for ttl in range(NTT):
                m_cw = p.emit("vector", "tensor_scalar", fv(C_CW + ttl * 16, 16),
                              fv(GT_CWU + ttl * 16, 16), fv(GT_WI + ttl, 1), None, ALU.mult,
                              _inc="dv")
            m_cw_box[0] = m_cw
        gstep(s_tail)
    p.wait("vector", m_gate_mm)
    if STOP == "gate":
        dbg_dump([("scores", fv(C_SCORES, 256)), ("mc", fv(C_MC, 16)),
                  ("vc", fv(C_VC, 16)), ("rsc", fv(C_RSC, 16)), ("gb2", fv(C_GB2, 16)),
                  ("mrow", fv(A_MROW, 2048, 0, 1)), ("vrow", fv(A_VROW, 2048, 0, 1)),
                  ("x1b", bv(B_X1B, 16384))])
        _materialize_stop()
        return nc

    p.wait("gpsimd", attn_done_mm)
    wgu_slot = [Res() for _ in range(6)]
    wd_slot = [Res() for _ in range(NIC)]
    silu_res = [Res(), Res()]
    st_e = {"m_down_last": None, "myd": None, "m_cw": None}

    def wsrc_gu(e, ic, which):
        if e < NEXP:
            t = g["gpT"] if which == 0 else g["upT"]
            return t[e][:, ic * 128:(ic + 1) * 128].rearrange("(dc p) i -> p dc i", p=128)
        hf = e - NEXP
        t = g["sgT"] if which == 0 else g["suT"]
        return t[:, hf * I_EXP + ic * 128: hf * I_EXP + (ic + 1) * 128].rearrange(
            "(dc p) i -> p dc i", p=128)

    def wsrc_d(e, ic):
        if e < NEXP:
            return g["dpT"][e][ic * 128:(ic + 1) * 128, :]
        hf = e - NEXP
        return g["sdT"][hf * I_EXP + ic * 128: hf * I_EXP + (ic + 1) * 128, :]

    dma_map = {}
    all_pairs = [(e, ic) for e in range(NPSEUDO) for ic in range(NIC)]

    def emit_gu_dma(pair_idx):
        e, ic = all_pairs[pair_idx]
        ms = []
        for which in range(2):
            sl_i = (pair_idx * 2 + which) % 6
            sl = wgu_slot[sl_i]
            p.wait("gpsimd", sl.r)
            ms.append((sl_i,
                       dma("gpsimd",
                           bv(B_WGU + sl_i * 1024, 1024).rearrange("p (dc i) -> p dc i", dc=8),
                           wsrc_gu(e, ic, which), "g%d" % sl_i)))
        dma_map[(e, ic)] = ms

    for i in range(3):
        emit_gu_dma(i)

    def plan_wd(e):
        wd_m = []
        for ic in range(NIC):
            p.wait("gpsimd", wd_slot[ic].r)
            wd_m.append(dma("gpsimd", bv(B_WD + ic * 1024, 1024), wsrc_d(e, ic), "d%d" % ic))
        return wd_m

    def plan_gu(e, steps=None):
        m_act_last = None
        for ic in range(NIC):
            if steps:
                for _ in range(4):
                    if steps:
                        steps.pop(0)()
            (slg, mg), (slu, mu) = dma_map.pop((e, ic))
            for half in range(2):
                t0, t1 = 2 * half, 2 * half + 1
                bset = 0 if (ic * 2 + half) % 2 == 0 else 4
                bg0, bg1 = banks[bset], banks[bset + 1]
                bu0, bu1 = banks[bset + 2], banks[bset + 3]
                for bk in (bg0, bg1, bu0, bu1):
                    p.wait("tensor", bk.r)
                p.wait("tensor", mg)
                p.wait("tensor", mu)
                for _xm in x1b_ready:
                    p.wait("tensor", _xm)
                for dc in range(NEC):
                    st, sp = (dc == 0), (dc == NEC - 1)
                    wgt = bv(B_WGU + slg * 1024 + dc * 128, 128)
                    wut = bv(B_WGU + slu * 1024 + dc * 128, 128)
                    x0 = bv(B_X1B + dc * 2048 + t0 * 512, 512)
                    x1v = bv(B_X1B + dc * 2048 + t1 * 512, 512)
                    p.emit("tensor", "matmul", bg0.ap[:, 0:512], wgt, x0, start=st, stop=sp,
                           _inc=("mm" if sp else None))
                    p.emit("tensor", "matmul", bg1.ap[:, 0:512], wgt, x1v, start=st, stop=sp,
                           _inc=("mm" if sp else None))
                    p.emit("tensor", "matmul", bu0.ap[:, 0:512], wut, x0, start=st, stop=sp,
                           _inc=("mm" if sp else None))
                    mker = p.emit("tensor", "matmul", bu1.ap[:, 0:512], wut, x1v, start=st,
                                  stop=sp, _inc=("mm" if sp else None))
                bg0.w = bg1.w = bu0.w = bu1.w = mker
                for bg, bu, tau in ((bg0, bu0, t0), (bg1, bu1, t1)):
                    sres = silu_res[tau % 2]
                    p.wait("scalar", bg.w)
                    if sres.r is not None:
                        p.wait("scalar", sres.r)
                    msil = p.emit("scalar", "activation", bv(B_SILU + (tau % 2) * 512, 512),
                                  bg.ap[:, 0:512], AF.Silu, _inc="ac")
                    bg.r = msil
                    sres.w = msil
                    p.wait("vector", bu.w)
                    p.wait("vector", msil)
                    if ic == 0 and half == 0 and tau == t0:
                        p.wait("vector", st_e["m_down_last"])
                    mact = p.emit("vector", "tensor_tensor",
                                  bv(B_ACT + ic * 2048 + tau * 512, 512),
                                  bu.ap[:, 0:512], bv(B_SILU + (tau % 2) * 512, 512), ALU.mult,
                                  _inc="dv")
                    bu.r = mact
                    sres.r = mact
                if half == 1:
                    wgu_slot[slg].r = mker
                    wgu_slot[slu].r = mker
                    if ic == NIC - 1:
                        m_act_last = mact
            nxt = e * NIC + ic + 3
            if nxt < len(all_pairs):
                emit_gu_dma(nxt)
        return m_act_last

    def plan_down(e, m_act_last, wd_m, tile_cb=None, first=False):
        myd = None
        for ttl in range(NTT):
            bset = 4 if ttl % 2 == 0 else 6
            b0, b1 = banks[bset], banks[bset + 1]
            p.wait("tensor", b0.r)
            p.wait("tensor", b1.r)
            p.wait("tensor", m_act_last)
            for _m in wd_m:
                p.wait("tensor", _m)
            p.wait("tensor", st_e["m_cw"])
            for ic in range(NIC):
                st, sp = (ic == 0), (ic == NIC - 1)
                a = bv(B_ACT + ic * 2048 + ttl * 128, 128)
                p.emit("tensor", "matmul", b0.ap[:, 0:512], a,
                       bv(B_WD + ic * 1024, 512), start=st, stop=sp,
                       _inc=("mm" if sp else None))
                mker = p.emit("tensor", "matmul", b1.ap[:, 0:512], a,
                              bv(B_WD + ic * 1024 + 512, 512), start=st, stop=sp,
                              _inc=("mm" if sp else None))
            b0.w = b1.w = mker
            for hh, bk in ((0, b0), (1, b1)):
                p.wait("vector", bk.w)
                if first and ttl == 0 and hh == 0:
                    p.wait("vector", spill_mark[0])
                    p.wait("vector", spill_mark[1])
                ysl = FB[:, ttl * 1024 + hh * 512: ttl * 1024 + hh * 512 + 512]
                if first:
                    myd = p.emit("vector", "tensor_scalar", ysl, bk.ap[:, 0:512],
                                 fv(C_CW + ttl * 16 + e, 1), None, ALU.mult, _inc="dv")
                elif e < NEXP:
                    myd = p.emit("vector", "scalar_tensor_tensor", ysl, bk.ap[:, 0:512],
                                 fv(C_CW + ttl * 16 + e, 1), ysl, ALU.mult, ALU.add, _inc="dv")
                else:
                    myd = p.emit("vector", "tensor_tensor", ysl, bk.ap[:, 0:512], ysl,
                                 ALU.add, _inc="dv")
                bk.r = myd
            if tile_cb is not None:
                tile_cb(ttl, myd)
        st_e["myd"] = myd
        st_e["m_down_last"] = p.marker("mm")
        for ic in range(NIC):
            wd_slot[ic].r = st_e["m_down_last"]

    plan_gate_logic()
    wd_m0 = plan_wd(0)
    m_act0 = plan_gu(0, gate_steps)
    while gate_steps:
        gate_steps.pop(0)()
    st_e["m_cw"] = m_cw_box[0]
    plan_down(0, m_act0, wd_m0, first=True)
    for e in range(1, NPSEUDO - 1):
        wd_m = plan_wd(e)
        m_act = plan_gu(e)
        plan_down(e, m_act, wd_m)

    xe_slot = [Res(), Res()]
    xe_slot[0].r = m_cw_box[0]
    xe_slot[1].r = m_cw_box[0]
    p.wait("sync", spill_mark[0])
    p.wait("sync", spill_mark[1])
    p.wait("sync", m_x1b_cp_last)
    outs_slot = [Res(), Res()]
    x1r = x1sp.rearrange("(dc p) t -> p dc t", p=128)

    L2C1S = C_SCORES
    L2C2S = C_SCORES + 16
    L2MS = C_SCORES + 32
    L2SDS = C_SCORES + 48
    end_state = {"sq": None, "mza": {}}

    def end_tile(ttl, m_ytile):
        sl = xe_slot[ttl % 2]
        p.wait("sync", sl.r)
        mxe = dma("sync", fv(E_XE + (ttl % 2) * 1024, 1024).rearrange("p (dc t) -> p dc t", dc=8),
                  x1r[:, :, ttl * 128:(ttl + 1) * 128], "xe%d" % (ttl % 2))
        mza = None
        mtr = None
        for dc in range(NEC):
            bk = banks[dc % 2]
            p.wait("tensor", bk.r)
            p.wait("tensor", mxe)
            p.wait("tensor", m_consts)
            mtr = p.emit("tensor", "transpose", bk.ap[:, 0:128],
                         fv(E_XE + (ttl % 2) * 1024 + dc * 128, 128), fv(C_IDT, 128), _inc="mm")
            bk.w = mtr
            ysl = FB[:, ttl * 1024 + dc * 128: ttl * 1024 + dc * 128 + 128]
            p.wait("vector", mtr)
            p.wait("vector", m_ytile)
            mza = p.emit("vector", "tensor_tensor", ysl, bk.ap[:, 0:128], ysl, ALU.add, _inc="dv")
            bk.r = mza
        sl.r = mtr
        YT = FB[:, ttl * 1024:(ttl + 1) * 1024]
        p.wait("scalar", mza)
        msq2 = p.emit("scalar", "activation", fv(E_SQ, 1024), YT, AF.Square,
                      accum_out=fv(L2C2S + ttl, 1), _inc="ac")
        p.emit("vector", "reduce_sum", fv(L2C1S + ttl, 1), YT, axis=AX.X, _inc="dv")
        end_state["sq"] = msq2
        end_state["mza"][ttl] = mza
        if ttl == 7:
            end_finish_range(0, 8)
        elif ttl == 15:
            end_finish_range(8, 16)

    def end_finish_range(lo, hi):
        n = hi - lo
        p.fence("vector", "dv")
        p.wait("vector", end_state["sq"])
        p.emit("vector", "tensor_scalar", fv(L2MS + lo, n), fv(L2C1S + lo, n), 1.0 / 1024.0,
               None, ALU.mult)
        p.fence("vector", "dv")
        p.emit("vector", "tensor_tensor", fv(L2SDS + lo, n), fv(L2MS + lo, n), fv(L2MS + lo, n),
               ALU.mult)
        p.fence("vector", "dv")
        mvar = p.emit("vector", "scalar_tensor_tensor", fv(L2SDS + lo, n), fv(L2C2S + lo, n),
                      1.0 / 1024.0, fv(L2SDS + lo, n), ALU.mult, ALU.subtract, _inc="dv")
        p.wait("scalar", mvar)
        msd = p.emit("scalar", "activation", fv(L2SDS + 16 + lo, n), fv(L2SDS + lo, n), AF.Sqrt,
                     bias=fv(C_EPSC, 1), _inc="ac")
        p.wait("vector", msd)
        p.emit("vector", "reciprocal", fv(L2SDS + 32 + lo, n), fv(L2SDS + 16 + lo, n))
        p.fence("vector", "dv")
        for ttl in range(lo, hi):
            YT = FB[:, ttl * 1024:(ttl + 1) * 1024]
            osl = outs_slot[ttl % 2]
            if osl.r is not None:
                p.wait("vector", osl.r)
            mo = p.emit("vector", "tensor_scalar", fv(E_OUTS + (ttl % 2) * 1024, 1024), YT,
                        fv(L2MS + ttl, 1), fv(L2SDS + 32 + ttl, 1), ALU.subtract, ALU.mult,
                        _inc="dv")
            p.wait("sync", mo)
            mst = dma("sync", out[ttl * 128:(ttl + 1) * 128, :],
                      fv(E_OUTS + (ttl % 2) * 1024, 1024), "ou%d" % (ttl % 2))
            osl.r = mst

    wd_m17 = plan_wd(NPSEUDO - 1)
    m_act17 = plan_gu(NPSEUDO - 1)
    plan_down(NPSEUDO - 1, m_act17, wd_m17, tile_cb=end_tile)
    if STOP == "fin":
        dbg_dump([("y", FB[:, 0:16384]), ("l2", fv(GS, 16)),
                  ("outs", fv(E_OUTS, 2048)), ("esq", fv(E_SQ, 1024)),
                  ("cw", fv(C_CW, 256))])
        _materialize_stop()
        return nc
    for snm in ["ou0", "ou1", "sp0", "sp1"] + ["rl%d" % i for i in range(NSC - 2)]:
        p.wait("sync", p.marker(snm))

    def runner(engname):
        def run(eng):
            for item in p.q[engname]:
                if item[0] == "w":
                    eng.wait_ge(sems[item[1]], item[2])
                else:
                    _, fn, args, kwargs, inc, incn = item
                    r = getattr(eng, fn)(*args, **kwargs)
                    if inc:
                        r.then_inc(sems[inc], incn)
        return run

    with nc.Block() as block:
        block.sync(runner("sync"))
        block.gpsimd(runner("gpsimd"))
        block.scalar(runner("scalar"))
        block.vector(runner("vector"))
        block.tensor(runner("tensor"))
    ctx.close()
    return nc


_NC_CACHE = None


def make_inmaps(inputs):
    f32 = np.float32
    x = np.asarray(inputs["L0_attn_x"], f32)
    tm = np.asarray(inputs["L0_attn_time_mix"], f32).reshape(-1)
    td = np.asarray(inputs["L0_attn_time_decay"], f32).reshape(-1)
    tf = np.asarray(inputs["L0_attn_time_first"], f32).reshape(-1)

    share = {}
    for nm, key in (("wkT", "L0_attn_Wk"), ("wvT", "L0_attn_Wv"), ("wrT", "L0_attn_Wr"),
                    ("wgT", "L0_attn_Wg"), ("woT", "L0_attn_Wo")):
        share[nm] = np.ascontiguousarray(np.asarray(inputs[key], f32).T)
    share["gwT"] = np.ascontiguousarray(np.asarray(inputs["L0_ffn_gate_w"], f32).T)
    share["gbrow"] = np.asarray(inputs["L0_ffn_gate_bias"], f32).reshape(1, 16)
    share["gpT"] = np.ascontiguousarray(np.asarray(inputs["L0_ffn_gate_proj"], f32).transpose(0, 2, 1))
    share["upT"] = np.ascontiguousarray(np.asarray(inputs["L0_ffn_up_proj"], f32).transpose(0, 2, 1))
    share["dpT"] = np.ascontiguousarray(np.asarray(inputs["L0_ffn_down_proj"], f32).transpose(0, 2, 1))
    share["sgT"] = np.ascontiguousarray(np.asarray(inputs["L0_ffn_shared_gate_w"], f32).T)
    share["suT"] = np.ascontiguousarray(np.asarray(inputs["L0_ffn_shared_up_w"], f32).T)
    share["sdT"] = np.ascontiguousarray(np.asarray(inputs["L0_ffn_shared_down_w"], f32).T)
    share["tdcT"] = np.ascontiguousarray(td.reshape(NEC, 128).T)
    share["tfcT"] = np.ascontiguousarray(tf.reshape(NEC, 128).T)
    share["tmcT"] = np.ascontiguousarray(tm.reshape(NEC, 128).T)
    share["tdrow"] = td.reshape(1, D).copy()
    ii = np.arange(128, dtype=f32)
    cra = np.where(ii < 64, -(ii - 31.5), -(ii - 95.5)).astype(f32)
    share["crpack"] = np.concatenate([
        np.broadcast_to(cra, (128, 128)),
        np.broadcast_to(-(95.5 - ii), (128, 128)).astype(f32),
        np.broadcast_to(-(ii + 1.0), (128, 128)).astype(f32),
        np.broadcast_to(-(31.5 - ii[:64]), (128, 64)).astype(f32)], axis=1).copy()
    share["cjc"] = (-(127.0 - ii)).astype(f32).reshape(128, 1)
    jj = np.arange(64, dtype=f32)
    share["ctril"] = (jj[:, None] <= jj[None, :]).astype(f32)
    share["ctrih"] = (ii[:, None] <= (jj[None, :] + 64.0)).astype(f32)
    share["ctrili"] = (jj[:, None] <= jj[None, :]).astype(np.uint8)
    share["ctrihi"] = (ii[:, None] <= (jj[None, :] + 64.0)).astype(np.uint8)
    share["ctrif"] = (ii[:, None] <= ii[None, :]).astype(np.uint8)
    share["cr16"] = np.broadcast_to(np.arange(16, dtype=f32) * DELTA, (128, 16)).copy()
    share["cgr4"] = np.broadcast_to(np.arange(4, dtype=f32) * DELTA, (128, 4)).copy()
    share["cidt"] = np.eye(128, dtype=f32)

    zeros_pre = np.zeros((D, TLOC), f32)
    in_maps = []
    for core in range(8):
        b, half = core // 2, core % 2
        xTb = np.ascontiguousarray(x[b].T)
        m = dict(share)
        m["xT"] = np.ascontiguousarray(xTb[:, half * TLOC:(half + 1) * TLOC])
        m["xTpre"] = np.ascontiguousarray(xTb[:, 0:TLOC]) if half == 1 else zeros_pre
        in_maps.append(m)
    return in_maps


def kernel(**inputs):
    global _NC_CACHE
    f32 = np.float32
    in_maps = make_inmaps(inputs)
    if _NC_CACHE is None:
        _NC_CACHE = build_nc()
    res = run_bass_kernel_spmd(_NC_CACHE, in_maps, core_ids=list(range(8)))
    outp = np.empty((4, 4096, D), f32)
    if DEBUG:
        kernel.debug_results = res.results
    for core in range(8):
        b, half = core // 2, core % 2
        outp[b, half * TLOC:(half + 1) * TLOC, :] = np.asarray(res.results[core]["out"])
    return outp
